# version 08
# baseline (speedup 1.0000x reference)
import os
import numpy as np
import ml_dtypes
from contextlib import ExitStack

import concourse.bass as bass
import concourse.mybir as mybir
from concourse import library_config
from concourse.libnrt import node_relative_to_routing_address
from concourse.bass_utils import run_bass_kernel_spmd

F32 = mybir.dt.float32
BF16 = mybir.dt.bfloat16
I32 = mybir.dt.int32
AF = mybir.ActivationFunctionType
ALU = mybir.AluOpType
AX = mybir.AxisListType
BF = ml_dtypes.bfloat16

D = 2048
TOK = 1024
NT = 8
KC = 16
MAGIC = 0x3E50000
SCALE_B = float(np.float32(0.08838834764831843))

NSEQ = 8
NCH = 16
_SUBBLOCKS = 4
_PHASES = 999
_LN_PLAIN = False
_REG_BYPASS = False
_FORCE_SIM_VARIANT = False
_CACHE = {}


class _StopEmit(Exception):
    pass


def _is_sim():
    return os.environ.get("BASS_SIM_FP64") == "1" or _FORCE_SIM_VARIANT


class Ctx:
    ENGS = ("t", "v", "s", "y", "g")

    def __init__(self, nc, stack):
        self.nc = nc
        self.stack = stack
        self.ops = {k: [] for k in self.ENGS}
        self._sems = {}
        self.cnt = {}
        self.regs = {}

    def sem(self, name):
        if name not in self._sems:
            self._sems[name] = self.stack.enter_context(self.nc.semaphore(name=name))
            self.cnt[name] = 0
        return self._sems[name]

    def on(self, eng, fn):
        self.ops[eng].append(fn)

    def wait(self, eng, name, v=None):
        s = self.sem(name)
        v = self.cnt[name] if v is None else v
        if v > 0:
            self.ops[eng].append(lambda e, s=s, v=v: e.wait_ge(s, v))

    SELF_SEM = {"v": "sv", "s": "ss"}

    def do(self, eng, fn, sem=None, n=1):
        if eng in self.SELF_SEM:
            name = self.SELF_SEM[eng]
            assert sem in (None, name), (eng, sem)
            s = self.sem(name)
            prev = self.cnt[name]
            if prev:
                self.ops[eng].append(lambda e, s=s, v=prev: e.wait_ge(s, v))
            self.cnt[name] = prev + 1
            self.ops[eng].append(lambda e, s=s, fn=fn: fn(e).then_inc(s, 1))
            return self.cnt[name]
        if sem is None:
            self.ops[eng].append(fn)
            return None
        s = self.sem(sem)
        self.cnt[sem] += n
        self.ops[eng].append(lambda e, s=s, n=n, fn=fn: fn(e).then_inc(s, n))
        return self.cnt[sem]

    def val(self, name):
        self.sem(name)
        return self.cnt[name]

    def barrier(self):
        vals = {}
        for en in self.ENGS:
            if en == "y":
                vals[en] = self.do(en, lambda e: e.nop(), "by", 1)
            elif en in self.SELF_SEM:
                vals[en] = self.do(en, lambda e: e.drain())
            else:
                vals[en] = self.do(en, lambda e: e.drain(), "b" + en, 1)
        for en in self.ENGS:
            for other in self.ENGS:
                name = {"y": "by", "v": "sv", "s": "ss"}.get(other, "b" + other)
                if other != en:
                    self.wait(en, name, vals[other])


def build_program(nsb, phases=None):
    nph = _PHASES if phases is None else phases
    nc = bass.Bass("TRN2", target_bir_lowering=False)
    stack = ExitStack()
    sim = _is_sim()

    dr = {}

    def din(name, shape, dt=F32):
        dr[name] = nc.dram_tensor(name, list(shape), dt, kind="ExternalInput").ap()

    din("x", (TOK, D))
    for w in ("aWq", "aWk", "aWv", "aWo", "bWq", "bWk", "bWv", "bWo", "fW"):
        din(w, (D, D))
    din("bWb", (D, 16))
    din("abo", (1, D))
    din("ident", (128, 128), BF16)
    din("trilmask", (128, 128), BF16)
    din("triumask", (128, 128), BF16)
    din("triuincl", (128, 128), BF16)
    din("trilu", (128, 256), BF16)
    din("esel", (32, KC * 128), BF16)
    din("ones1x", (1, 128), BF16)
    din("onescol", (128, 1), BF16)
    din("ropec", (128, NT * 64), BF16)
    din("ropes", (128, NT * 64), BF16)
    din("role", (1, 1), I32)
    din("ppnc", (1, 1), I32)
    y_out = nc.dram_tensor("y", [TOK, D], F32, kind="ExternalOutput").ap()

    ln_hbm = nc.dram_tensor("ln_hbm", [TOK, D], BF16, kind="Internal").ap()
    wq_hbm = nc.dram_tensor("wq_hbm", [D, D], BF16, kind="Internal").ap()
    shm_kv = nc.dram_tensor("shm_kv", [2, 128, 1040], F32, kind="Internal", addr_space="Shared").ap()
    shm_q = nc.dram_tensor("shm_q", [2, TOK, D], BF16, kind="Internal", addr_space="Shared").ap()
    shm_k = nc.dram_tensor("shm_k", [2, TOK, D], BF16, kind="Internal", addr_space="Shared").ap()
    shm_v = nc.dram_tensor("shm_v", [2, TOK, D], BF16, kind="Internal", addr_space="Shared").ap()
    shm_br = nc.dram_tensor("shm_br", [2, 16, TOK], BF16, kind="Internal", addr_space="Shared").ap()
    shm_bt = nc.dram_tensor("shm_bt", [2, TOK, 16], F32, kind="Internal", addr_space="Shared").ap()
    shm_o = nc.dram_tensor("shm_o", [2, 8, 128, 2 * TOK], BF16, kind="Internal", addr_space="Shared").ap()
    shm_fl = nc.dram_tensor("shm_fl", [2, 16], I32, kind="Internal", addr_space="Shared").ap()

    sb = {}

    def sbuf(name, shape, dt):
        sb[name] = stack.enter_context(nc.sbuf_tensor(name, list(shape), dt))
        return sb[name]

    xres = sbuf("xres", (128, NT * D), F32)
    fmA = sbuf("fmA", (128, KC * TOK), BF16)
    fmB = sbuf("fmB", (128, KC * TOK), BF16)
    wbuf = sbuf("wbuf", (128, 2 * KC * 512), BF16)
    scr = sbuf("scr", (128, 4096), F32)
    scr2 = sbuf("scr2", (128, 1024), F32)
    cols = sbuf("cols", (128, 64), F32)
    lnt = sbuf("lnt", (128, 2 * D), BF16)
    f32ones = sbuf("f32ones", (1, 128), F32)
    f32onescol = sbuf("f32onescol", (128, 1), F32)
    inva = sbuf("inva", (1, 1), F32)
    kvf = sbuf("kvf", (128, 1024), BF16)
    ksumS = sbuf("ksumS", (128, KC * 32), BF16)
    nrmvb = sbuf("nrmvb", (32, 1024), BF16)
    c_id = sbuf("c_id", (128, 128), BF16)
    c_tril = sbuf("c_tril", (128, 128), BF16)
    c_triu = sbuf("c_triu", (128, 128), BF16)
    c_triu_incl = sbuf("c_triu_incl", (128, 128), BF16)
    c_trilu = sbuf("c_trilu", (128, 256), BF16)
    c_esel = sbuf("c_esel", (32, KC * 128), BF16)
    c_ones1x = sbuf("c_ones1x", (1, 128), BF16)
    c_onescol = sbuf("c_onescol", (128, 1), BF16)
    c_ropec = sbuf("c_ropec", (128, NT * 64), BF16)
    c_ropes = sbuf("c_ropes", (128, NT * 64), BF16)
    c_bo = sbuf("c_bo", (1, D), BF16)
    c_wb = sbuf("c_wb", (128, KC * 16), BF16)
    srole = sbuf("srole", (1, 1), I32)
    sflag = sbuf("sflag", (1, 16), I32)
    sppnc = sbuf("sppnc", (1, 1), I32)
    sdtx = sbuf("sdtx", (128, 1), F32)
    sdrx = sbuf("sdrx", (128, 1), F32)

    kvb = scr[:, 0:1040]
    betat = scr2[:, 128:384]
    betacol = scr2[:, 384:400]
    kvp = scr[:, 1040:2080]
    nrmv = scr2[0:32, 0:1024]
    nrow = scr[0:1, 2048:4096]
    nrowb = lnt[0:1, 2048:4096]
    betarow = lnt[0:1, 0:2048]

    pb = [stack.enter_context(nc.psum_tensor(f"pb{_i}", [128, 512], F32)) for _i in range(7)]

    ctx = Ctx(nc, stack)
    phc = [0]

    def phase():
        phc[0] += 1
        if phc[0] >= nph:
            raise _StopEmit()
    if sim:
        pass

    A_qhT = fmA[:, 0:2048]
    A_khT = fmA[:, 2048:4096]
    A_ktT = fmA[:, 4096:6144]
    A_ktok = fmA[:, 6144:8192]
    A_kttok = fmA[:, 8192:10240]
    A_vtok = fmA[:, 10240:12288]
    A_vn = fmA[:, 12288:14336]
    REGOFF = 8192

    def B_exp(kt):
        return fmB[:, 512 * kt:512 * kt + 512]

    def B_reg(j):
        return fmB[:, REGOFF + 128 * j:REGOFF + 128 * j + 128]

    def B_X(p):
        return fmB[:, REGOFF + 13 * 128 + 128 * p:REGOFF + 13 * 128 + 128 * p + 128]

    B_S16 = fmB[:, 13184:13312]
    B_oT = fmB[:, 13312:15360]
    S32 = scr2[:, 0:128]
    RMAGIC = 12582912.0

    def C(j):
        return cols[:, j:j + 1]

    def dma(eng, out, in_, sem="qy", transpose=False):
        if transpose:
            return ctx.do(eng, lambda e, o=out, i=in_: e.dma_start(out=o, in_=i, transpose=True), sem, 16)
        return ctx.do(eng, lambda e, o=out, i=in_: e.dma_start(out=o, in_=i), sem, 16)

    def role_dma(eng, p0, p1, sem="qy"):
        assert len(p0) == len(p1)
        s = ctx.sem(sem)
        ctx.cnt[sem] += 16 * len(p0)

        def f(e, p0=p0, p1=p1, s=s, eng=eng):
            rr = ctx.regs[eng + "_role"]
            with e.If_eq(rr, 0):
                for (o, i, tr) in p0:
                    ins = e.dma_start(out=o, in_=i, transpose=True) if tr else e.dma_start(out=o, in_=i)
                    ins.then_inc(s, 16)
            with e.Else():
                for (o, i, tr) in p1:
                    ins = e.dma_start(out=o, in_=i, transpose=True) if tr else e.dma_start(out=o, in_=i)
                    ins.then_inc(s, 16)
            return None
        ctx.on(eng, f)
        return ctx.cnt[sem]

    sync_round = [0]

    def pair_sync():
        sync_round[0] += 1
        rnd = sync_round[0]
        if sim:
            rs = ctx.sem(f"rsem{rnd}")
            ls = ctx.sem(f"lsem{rnd}")
            pp = ctx.sem(f"prp{rnd}")
            def f(e, rs=rs, ls=ls, pp=pp):
                p = ctx.regs["g_ppnc"]
                e.remote_dma(sdrx[:], sdtx[:], rs, ls, p, 0, 0x10).then_inc(pp, 1)
                e.wait_ge(pp, 1)
                e.trigger_dma()
                return e.wait_ge(ls, 1)
            ctx.do("g", f, "sg", 1)
            return (f"rsem{rnd}", 1)
        vv = ctx.do("v", lambda e, r=rnd: e.memset(sflag[:], MAGIC + r), "sv", 1)
        ctx.wait("g", "sv", vv)
        qgf = ctx.sem("qgf")
        ctx.cnt["qgf"] += 16
        tgt = ctx.cnt["qgf"]

        def f(e, rnd=rnd, tgt=tgt):
            rr = ctx.regs["g_role"]
            with e.If_eq(rr, 0):
                e.dma_start(out=shm_fl[0:1, :], in_=sflag[:]).then_inc(qgf, 16)
            with e.Else():
                e.dma_start(out=shm_fl[1:2, :], in_=sflag[:]).then_inc(qgf, 16)
            e.wait_ge(qgf, tgt)
            f1 = e.alloc_register()
            nd = e.alloc_register()
            with e.If_eq(rr, 0):
                e.load(f1, shm_fl[1:2, 0:1])
                e.reg_alu(nd, f1, MAGIC + rnd, ALU.not_equal)
                with e.While(nd):
                    e.load(f1, shm_fl[1:2, 0:1])
                    e.reg_alu(nd, f1, MAGIC + rnd, ALU.not_equal)
            with e.Else():
                e.load(f1, shm_fl[0:1, 0:1])
                e.reg_alu(nd, f1, MAGIC + rnd, ALU.not_equal)
                with e.While(nd):
                    e.load(f1, shm_fl[0:1, 0:1])
                    e.reg_alu(nd, f1, MAGIC + rnd, ALU.not_equal)
            return e.nop()
        ctx.do("g", f, "sg", 1)
        return ("sg", ctx.val("sg"))

    def load_wcb(wname, nsl, d):
        src = dr[wname][:, 512 * nsl:512 * (nsl + 1)].rearrange("(k p) f -> p k f", p=128)
        dst = wbuf[:, d * 8192:(d + 1) * 8192].rearrange("p (k f) -> p k f", k=KC)
        return dma("g", dst, src, f"qg{d}")

    def wslice(d, k, lo=0, width=512):
        return wbuf[:, d * 8192 + k * 512 + lo:d * 8192 + k * 512 + lo + width]

    def mme(bank_ap, lhsT, rhs, start, stop, inc=False):
        fn = lambda e, b=bank_ap, l=lhsT, r=rhs, st=start, sp=stop: \
            e.matmul(b, l, r, start=st, stop=sp, skip_group_check=True)
        if inc:
            return ctx.do("t", fn, "st", 1)
        ctx.do("t", fn)
        return None

    def tmark():
        s = ctx.sem("st")
        ctx.cnt["st"] += 1
        last = ctx.ops["t"].pop()
        ctx.ops["t"].append(lambda e, f=last, s=s: f(e).then_inc(s, 1))
        return ctx.cnt["st"]

    ctx.sem("qy_role")
    ctx.cnt["qy_role"] = 32 if sim else 16
    ctx.do("v", lambda e: e.memset(C(43), 1e-5))
    if sim:
        ctx.do("v", lambda e: e.memset(sdtx[:], 1.0))
    ctx.do("v", lambda e: e.memset(C(44), 1e-6), "sv", 1)
    for i in range(NT):
        dma("y", xres[:, i * D:(i + 1) * D], dr["x"][128 * i:128 * (i + 1), :], "qy")
    for nm, t in (("ident", c_id), ("trilmask", c_tril), ("triumask", c_triu), ("triuincl", c_triu_incl),
                  ("trilu", c_trilu),
                  ("esel", c_esel), ("ones1x", c_ones1x), ("onescol", c_onescol),
                  ("ropec", c_ropec), ("ropes", c_ropes)):
        dma("y", t[:], dr[nm][:], "qy")
    dma("g", c_bo[:], dr["abo"][:], "qg")
    dma("g", c_wb[:].rearrange("p (k h) -> p k h", k=KC),
        dr["bWb"][:].rearrange("(k p) h -> p k h", p=128), "qg")
    for en in ("t", "v", "s", "g"):
        ctx.wait(en, "qy")
        ctx.wait(en, "qg")
    ctx.barrier()


    def ln_stage(ffn_mode):
        ffn_mode = ffn_mode and not _LN_PLAIN
        ctx.wait("s", "qy")
        ctx.wait("v", "qy")
        for _en in ("s", "v"):
            ctx.wait(_en, "qp0")
            ctx.wait(_en, "qp1")
        prevts = [0]
        for i in range(NT):
            xt = xres[:, i * D:(i + 1) * D]
            if prevts[0]:
                ctx.wait("s", "sv", prevts[0])
            sqv = ctx.do("s", lambda e, xt=xt: e.activation(scr[:, 0:2048], xt, AF.Square,
                                                            accum_out=C(36)), "ss", 1)
            ctx.do("v", lambda e, xt=xt: e.tensor_reduce(C(35), xt, AX.X, ALU.add))
            ctx.wait("v", "ss", sqv)
            ctx.do("v", lambda e: e.tensor_scalar(C(32), C(35), 1.0 / D, None, ALU.mult))
            ctx.do("v", lambda e: e.tensor_tensor(C(33), C(32), C(32), ALU.mult))
            vv = ctx.do("v", lambda e: e.scalar_tensor_tensor(C(34), C(36), 1.0 / D, C(33), ALU.mult, ALU.subtract), "sv", 1)
            ctx.wait("s", "sv", vv)
            sv2 = ctx.do("s", lambda e: e.activation(C(34), C(34), AF.Sqrt, bias=C(43)), "ss", 1)
            ctx.wait("v", "ss", sv2)
            ctx.do("v", lambda e, i=i: e.reciprocal(C(i), C(34)))
            vv = ctx.do("v", lambda e, i=i: e.scalar_tensor_tensor(C(8 + i), C(32), -1.0, C(i), ALU.mult, ALU.mult), "sv", 1)
            half = (i % 2) * D
            if not ffn_mode:
                ctx.wait("s", "sv", vv)
                if i >= 2:
                    ctx.wait("s", f"ql{i % 2}", stv[i % 2])
                ssv = ctx.do("s", lambda e, xt=xt, i=i, half=half: e.activation(
                    lnt[:, half:half + D], xt, AF.Identity, bias=C(8 + i), scale=C(i)), "ss", 1)
                ctx.wait("y", "ss", ssv)
                stv[i % 2] = dma("y", ln_hbm[128 * i:128 * (i + 1), :], lnt[:, half:half + D], f"ql{i % 2}")
            else:
                ctx.wait("s", "sv", vv)
                uv = scr[:, 2048:4096]
                ssv = ctx.do("s", lambda e, xt=xt, uv=uv, i=i: e.activation(
                    uv, xt, AF.Identity, bias=C(8 + i), scale=C(i)), "ss", 1)
                ctx.wait("v", "ss", ssv)
                ctx.do("v", lambda e, uv=uv: e.tensor_reduce(C(37), uv, AX.X, ALU.max,
                                                             apply_absolute_value=True))
                ctx.do("v", lambda e, i=i: e.tensor_scalar(C(16 + i), C(37), 1.0 / 127.0, 1e-8, ALU.mult, ALU.max))
                ctx.do("v", lambda e, i=i: e.tensor_tensor(C(24 + i), C(16 + i), C(41), ALU.mult))
                ctx.do("v", lambda e, i=i: e.reciprocal(C(38), C(16 + i)))
                ctx.do("v", lambda e, uv=uv: e.tensor_scalar(
                    scr[:, 0:2048], uv, C(38), RMAGIC, ALU.mult, ALU.add))
                if i >= 2:
                    ctx.wait("v", f"ql{i % 2}", stv[i % 2])
                vv = ctx.do("v", lambda e, half=half: e.tensor_scalar(
                    lnt[:, half:half + D], scr[:, 0:2048], -RMAGIC, None, ALU.add), "sv", 1)
                prevts[0] = vv
                ctx.wait("y", "sv", vv)
                stv[i % 2] = dma("y", ln_hbm[128 * i:128 * (i + 1), :], lnt[:, half:half + D], f"ql{i % 2}")
        phase()
        ctx.wait("y", "qy")
        ctx.wait("y", "ql0")
        ctx.wait("y", "ql1")
        for k in range(KC):
            dma("y", fmA[:, k * TOK:(k + 1) * TOK], ln_hbm[:, 128 * k:128 * (k + 1)], "qy", transpose=True)
        ctx.wait("t", "qy")
        ctx.wait("v", "qy")
        ctx.wait("s", "qy")
        ctx.barrier()
        phase()

    stv = [0, 0]

    def proj_tok(wname, lhsT_arena, evac, extra_mm=None, wq_transpose=False):
        wpend = [0, 0]
        evfree = [0, 0]
        if wq_transpose:
            ctx.wait("y", "qp0")
            ctx.wait("y", "qp1")
        for nsl in range(4):
            d = nsl % 2
            weng = "y" if wq_transpose else "g"
            if nsl >= 2:
                ctx.wait(weng, "st", wpend[d])
            if wq_transpose:
                for k in range(KC):
                    src = wq_hbm[512 * nsl:512 * (nsl + 1), 128 * k:128 * (k + 1)]
                    qv = dma("y", wslice(d, k), src, f"qw{d}", transpose=True)
                ctx.wait("t", f"qw{d}", qv)
            else:
                qv = load_wcb(wname, nsl, d)
                ctx.wait("t", f"qg{d}", qv)
            for tt in range(NT):
                bank = pb[tt % 2]
                if evfree[tt % 2]:
                    ctx.wait("t", "sv", evfree[tt % 2])
                first = True
                if extra_mm is not None:
                    ctx.do("t", lambda e, b=bank, nsl=nsl: extra_mm(e, b, nsl))
                    first = False
                for k in range(KC):
                    mme(bank[:], lhsT_arena[:, k * TOK + 128 * tt:k * TOK + 128 * tt + 128],
                        wslice(d, k), start=(first and k == 0), stop=(k == KC - 1))
                tv = tmark()
                wpend[d] = tv
                ctx.wait("v", "st", tv)
                evfree[tt % 2] = evac(tt, nsl, bank)
        ctx.barrier()

    def attn_a():
        phase()
        ln_stage(False)
        wpend = [0, 0]
        evfree = [0, 0]
        for nsl in range(4):
            d = nsl % 2
            if nsl >= 2:
                ctx.wait("g", "st", wpend[d])
            qv = load_wcb("aWk", nsl, d)
            ctx.wait("t", f"qg{d}", qv)
            for tt in range(NT):
                bank = pb[tt % 2]
                if evfree[tt % 2]:
                    ctx.wait("t", "ss", evfree[tt % 2])
                for k in range(KC):
                    mme(bank[:], fmA[:, k * TOK + 128 * tt:k * TOK + 128 * tt + 128],
                        wslice(d, k), start=(k == 0), stop=(k == KC - 1))
                tv = tmark()
                wpend[d] = tv
                ctx.wait("s", "st", tv)
                evfree[tt % 2] = ctx.do("s", lambda e, b=bank, tt=tt, nsl=nsl:
                                        e.activation(fmB[:, tt * D + 512 * nsl:tt * D + 512 * nsl + 512],
                                                     b[:], AF.Relu), "ss", 1)
        ctx.barrier()
        phase()
        wpend = [0, 0]
        evfree = [0, 0]
        vsl = scr2[:, 512:1024].bitcast(BF16)
        ctx.do("v", lambda e: e.memset(kvb, 0.0), "sv", 1)
        kvfree = 0
        for nsl in range(4):
            d = nsl % 2
            if nsl >= 2:
                ctx.wait("g", "st", wpend[d])
            qv = load_wcb("aWv", nsl, d)
            ctx.wait("t", f"qg{d}", qv)
            for tt in range(NT):
                bank = pb[tt % 2]
                if evfree[tt % 2]:
                    ctx.wait("t", "sv", evfree[tt % 2])
                for k in range(KC):
                    mme(bank[:], fmA[:, k * TOK + 128 * tt:k * TOK + 128 * tt + 128],
                        wslice(d, k), start=(k == 0), stop=(k == KC - 1))
                tv = tmark()
                ctx.wait("v", "st", tv)
                vofs = (tt % 2) * 512
                evfree[tt % 2] = ctx.do("v", lambda e, b=bank, vofs=vofs:
                                        e.tensor_copy(vsl[:, vofs:vofs + 512], b[:]), "sv", 1)
                ctx.wait("t", "sv", evfree[tt % 2])
                if kvfree:
                    ctx.wait("t", "sv", kvfree)
                for j in range(8):
                    h = nsl * 8 + j
                    m = h // 2
                    half = 64 * (h % 2)
                    kv_bank = pb[4] if m < 8 else pb[5]
                    mc = 64 * (m % 8)
                    mme(kv_bank[half:half + 64, mc:mc + 64],
                        fmB[:, tt * D + 64 * h:tt * D + 64 * h + 64],
                        vsl[:, vofs + 64 * j:vofs + 64 * j + 64],
                        start=True, stop=True)
                    mme(pb[6][half:half + 64, m:m + 1],
                        fmB[:, tt * D + 64 * h:tt * D + 64 * h + 64],
                        c_onescol[:], start=True, stop=True)
                tv2 = tmark()
                wpend[d] = tv2
                ctx.wait("v", "st", tv2)
                ko = 256 * (nsl % 2)
                kb = 512 if nsl >= 2 else 0
                pbk = pb[5] if nsl >= 2 else pb[4]
                ctx.do("v", lambda e, kb=kb, ko=ko, pbk=pbk: e.tensor_tensor(
                    kvb[:, kb + ko:kb + ko + 256], kvb[:, kb + ko:kb + ko + 256],
                    pbk[:, ko:ko + 256], ALU.add))
                ms = 4 * nsl
                kvfree = ctx.do("v", lambda e, ms=ms: e.tensor_tensor(
                    kvb[:, 1024 + ms:1024 + ms + 4], kvb[:, 1024 + ms:1024 + ms + 4],
                    pb[6][:, ms:ms + 4], ALU.add), "sv", 1)
        ctx.wait("y", "sv")
        role_dma("y", [(shm_kv[0], kvb, False)], [(shm_kv[1], kvb, False)], sem="qkv")
        ctx.wait("g", "qy")
        ctx.wait("g", "qkv")
        sgn, sgv = pair_sync()
        ctx.wait("y", sgn, sgv)
        role_dma("y", [(kvp, shm_kv[1], False)], [(kvp, shm_kv[0], False)])
        ctx.wait("v", "qy")
        ctx.wait("v", "qkv")
        ctx.do("v", lambda e: e.tensor_tensor(kvb, kvb, kvp, ALU.add))
        ctx.do("v", lambda e: e.tensor_copy(kvf[:], kvb[:, 0:1024]))
        ctx.do("v", lambda e: e.memset(ksumS[:], 0.0))
        for ch in range(KC):
            ctx.do("v", lambda e, ch=ch: e.tensor_copy(
                ksumS[0:64, 32 * ch + 2 * ch:32 * ch + 2 * ch + 1],
                kvb[0:64, 1024 + ch:1024 + ch + 1]))
            ctx.do("v", lambda e, ch=ch: e.tensor_copy(
                ksumS[64:128, 32 * ch + 2 * ch + 1:32 * ch + 2 * ch + 2],
                kvb[64:128, 1024 + ch:1024 + ch + 1]))
        ctx.do("v", lambda e: e.drain(), "sv", 1)
        ctx.barrier()
        phase()
        wpend = [0, 0]
        evfree2 = [0, 0, 0, 0]
        for ocb in range(4):
            d = ocb % 2
            if ocb >= 2:
                ctx.wait("g", "st", wpend[d])
            qv = load_wcb("aWq", ocb, d)
            ctx.wait("t", f"qg{d}", qv)
            for tokg in range(2):
                for oc in range(4):
                    bank = pb[oc]
                    if evfree2[oc]:
                        ctx.wait("t", "ss", evfree2[oc])
                    for k in range(KC):
                        mme(bank[:], wslice(d, k, 128 * oc, 128),
                            fmA[:, k * TOK + 512 * tokg:k * TOK + 512 * tokg + 512],
                            start=(k == 0), stop=(k == KC - 1))
                tv = tmark()
                wpend[d] = tv
                ctx.wait("s", "st", tv)
                for oc in range(4):
                    chn = ocb * 4 + oc
                    evfree2[oc] = ctx.do("s", lambda e, oc=oc, chn=chn, tokg=tokg:
                                         e.activation(fmB[:, chn * TOK + 512 * tokg:chn * TOK + 512 * tokg + 512],
                                                      pb[oc][:], AF.Relu), "ss", 1)
        ctx.barrier()
        phase()
        for tokg in range(2):
            for k in range(KC):
                mme(pb[4][0:32, :], ksumS[:, 32 * k:32 * k + 32],
                    fmB[:, k * TOK + 512 * tokg:k * TOK + 512 * tokg + 512],
                    start=(k == 0), stop=(k == KC - 1))
            tv = tmark()
            ctx.wait("v", "st", tv)
            vv = ctx.do("v", lambda e, tokg=tokg: e.tensor_scalar(
                nrmv[:, 512 * tokg:512 * tokg + 512], pb[4][0:32, :], 1e-6, None, ALU.max), "sv", 1)
            ctx.wait("t", "sv", vv)
            ctx.wait("s", "sv", vv)
            ctx.do("s", lambda e, tokg=tokg: e.activation(
                nrmv[:, 512 * tokg:512 * tokg + 512], nrmv[:, 512 * tokg:512 * tokg + 512], AF.Ln))
            sv1 = ctx.do("s", lambda e, tokg=tokg: e.activation(
                nrmv[:, 512 * tokg:512 * tokg + 512], nrmv[:, 512 * tokg:512 * tokg + 512], AF.Exp, scale=-1.0), "ss", 1)
            ctx.wait("v", "ss", sv1)
            ctx.do("v", lambda e, tokg=tokg: e.tensor_copy(
                nrmvb[:, 512 * tokg:512 * tokg + 512], nrmv[:, 512 * tokg:512 * tokg + 512]), "sv", 1)
        ctx.barrier()
        phase()
        zfree = [0, 0]
        for ch in range(KC):
            for tokg in range(2):
                bank = pb[tokg]
                nbank = pb[2 + tokg]
                if zfree[tokg]:
                    ctx.wait("t", "sv", zfree[tokg])
                mme(bank[0:64, :], kvf[0:64, 64 * ch:64 * ch + 64],
                    fmB[0:64, ch * TOK + 512 * tokg:ch * TOK + 512 * tokg + 512],
                    start=True, stop=True)
                mme(bank[64:128, :], kvf[64:128, 64 * ch:64 * ch + 64],
                    fmB[64:128, ch * TOK + 512 * tokg:ch * TOK + 512 * tokg + 512],
                    start=True, stop=True)
                tv = ctx.do("t", lambda e, ch=ch, tokg=tokg, nb=nbank:
                            e.matmul(nb[:], c_esel[:, 128 * ch:128 * ch + 128],
                                     nrmvb[:, 512 * tokg:512 * tokg + 512],
                                     start=True, stop=True, skip_group_check=True), "st", 1)
                ctx.wait("v", "st", tv)
                nbs = lnt[:, 512 * tokg:512 * tokg + 512]
                ctx.do("v", lambda e, nb=nbank, nbs=nbs: e.tensor_copy(nbs, nb[:]))
                zfree[tokg] = ctx.do("v", lambda e, b=bank, ch=ch, tokg=tokg, nbs=nbs:
                                     e.tensor_tensor(fmB[:, ch * TOK + 512 * tokg:ch * TOK + 512 * tokg + 512],
                                                     b[:], nbs, ALU.mult), "sv", 1)
        ctx.barrier()
        phase()
        def bo_mm(e, b, nsl):
            return e.matmul(b[:], c_ones1x[:], c_bo[:, 512 * nsl:512 * nsl + 512],
                            start=True, stop=False, skip_group_check=True)

        def evac_a(tt, nsl, bank):
            return ctx.do("v", lambda e, b=bank, tt=tt, nsl=nsl:
                          e.scalar_tensor_tensor(xres[:, tt * D + 512 * nsl:tt * D + 512 * nsl + 512],
                                                 b[:], 1.0, xres[:, tt * D + 512 * nsl:tt * D + 512 * nsl + 512],
                                                 ALU.mult, ALU.add), "sv", 1)
        proj_tok("aWo", fmB, evac_a, extra_mm=bo_mm)

    def wq_prep():
        phase()
        ctx.do("v", lambda e: e.memset(C(40), 0.0))
        ctx.do("v", lambda e: e.memset(f32ones[:], 1.0))
        ctx.do("v", lambda e: e.memset(f32onescol[:], 1.0))
        qprev = [0, 0]
        for k in range(KC):
            d = k % 2
            half = scr[:, 2048 * d:2048 * d + 2048]
            if k >= 2:
                ctx.wait("g", "sv", qprev[d])
            qv = dma("g", half, dr["fW"][128 * k:128 * (k + 1), :], f"qf{d}")
            ctx.wait("v", f"qf{d}", qv)
            ctx.do("v", lambda e, half=half: e.tensor_reduce(C(41), half, AX.X, ALU.add, apply_absolute_value=True))
            qprev[d] = ctx.do("v", lambda e: e.tensor_tensor(C(40), C(40), C(41), ALU.add), "sv", 1)
        vv = ctx.do("v", lambda e: e.drain(), "sv", 1)
        ctx.wait("t", "sv", vv)
        tv = ctx.do("t", lambda e: e.matmul(pb[6][0:1, 0:1], C(40), f32onescol[:],
                                            start=True, stop=True, skip_group_check=True), "st", 1)
        ctx.wait("v", "st", tv)
        ctx.do("v", lambda e: e.tensor_scalar(inva[:], pb[6][0:1, 0:1], 1.0 / (D * D), 1e-8, ALU.mult, ALU.max))
        vv = ctx.do("v", lambda e: e.reciprocal(inva[:], inva[:]), "sv", 1)
        ctx.wait("t", "sv", vv)
        tv = ctx.do("t", lambda e: e.matmul(pb[6][:, 0:1], f32ones[:], inva[:],
                                            start=True, stop=True, skip_group_check=True), "st", 1)
        ctx.wait("v", "st", tv)
        ctx.do("v", lambda e: e.tensor_copy(C(42), pb[6][:, 0:1]))
        vv = ctx.do("v", lambda e: e.reciprocal(C(41), C(42)), "sv", 1)
        ctx.wait("g", "sv", vv)
        qprev = [0, 0]
        sprev = [0, 0]
        for k in range(KC):
            d = k % 2
            half = scr[:, 2048 * d:2048 * d + 2048]
            if k >= 2:
                ctx.wait("g", "sv", qprev[d])
            qv = dma("g", half, dr["fW"][128 * k:128 * (k + 1), :], f"qf{d}")
            ctx.wait("v", f"qf{d}", qv)
            if k >= 2:
                ctx.wait("v", f"qp{d}", sprev[d])
            ctx.do("v", lambda e, half=half: e.tensor_scalar(half, half, C(42), RMAGIC, ALU.mult, ALU.add))
            ctx.do("v", lambda e, half=half: e.tensor_scalar(half, half, -RMAGIC, None, ALU.add))
            qprev[d] = ctx.do("v", lambda e, half=half, d=d: e.tensor_scalar(
                lnt[:, d * D:d * D + D], half, -1.0, 1.0, ALU.max, ALU.min), "sv", 1)
            ctx.wait("y", "sv", qprev[d])
            sprev[d] = dma("y", wq_hbm[128 * k:128 * (k + 1), :], lnt[:, d * D:d * D + D], f"qp{d}")
        ctx.barrier()
        phase()

    def ffn():
        ln_stage(True)

        def evac_f(tt, nsl, bank):
            return ctx.do("v", lambda e, b=bank, tt=tt, nsl=nsl:
                          e.scalar_tensor_tensor(xres[:, tt * D + 512 * nsl:tt * D + 512 * nsl + 512],
                                                 b[:], C(24 + tt), xres[:, tt * D + 512 * nsl:tt * D + 512 * nsl + 512],
                                                 ALU.mult, ALU.add), "sv", 1)
        proj_tok(None, fmA, evac_f, wq_transpose=True)

    def attn_b():
        ln_stage(False)
        ropev = lambda tt: c_ropec[:, 64 * tt:64 * tt + 64].unsqueeze(1).broadcast_to([128, 4, 64])
        ropesv = lambda tt: c_ropes[:, 64 * tt:64 * tt + 64].unsqueeze(1).broadcast_to([128, 4, 64])

        def h4(ap):
            return ap.rearrange("p (h d) -> p h d", h=4)

        stq = [0, 0]
        for wname, target, kind in (("bWv", shm_v, "v"), ("bWk", shm_k, "k"), ("bWq", shm_q, "q")):
            wpend = [0, 0]
            evfree = [0, 0]
            for nsl in range(4):
                d = nsl % 2
                if nsl >= 2:
                    ctx.wait("g", "st", wpend[d])
                qv = load_wcb(wname, nsl, d)
                ctx.wait("t", f"qg{d}", qv)
                for tt in range(NT):
                    bank = pb[tt % 2]
                    if evfree[tt % 2]:
                        ctx.wait("t", "sv", evfree[tt % 2])
                    if stq[tt % 2]:
                        ctx.wait("v", f"qs{tt % 2}", stq[tt % 2])
                        ctx.wait("s", f"qs{tt % 2}", stq[tt % 2])
                    for k in range(KC):
                        mme(bank[:], fmA[:, k * TOK + 128 * tt:k * TOK + 128 * tt + 128],
                            wslice(d, k), start=(k == 0), stop=(k == KC - 1))
                    tv = tmark()
                    wpend[d] = tv
                    sA = lnt[:, (tt % 2) * D:(tt % 2) * D + 512]
                    sB = lnt[:, (tt % 2) * D + 512:(tt % 2) * D + 1024]
                    sC = lnt[:, (tt % 2) * D + 1024:(tt % 2) * D + 1536]
                    sD = lnt[:, (tt % 2) * D + 1536:(tt % 2) * D + 2048]
                    if kind == "v":
                        ctx.wait("v", "st", tv)
                        vv = ctx.do("v", lambda e, b=bank, sA=sA: e.tensor_copy(sA, b[:]), "sv", 1)
                        evfree[tt % 2] = vv
                        ctx.wait("y", "sv", vv)
                        stq[tt % 2] = role_dma("y",
                                 [(target[0][128 * tt:128 * tt + 128, 512 * nsl:512 * nsl + 512], sA, False)],
                                 [(target[1][128 * tt:128 * tt + 128, 512 * nsl:512 * nsl + 512], sA, False)],
                                 sem=f"qs{tt % 2}")
                    else:
                        if kind == "k":
                            ctx.wait("s", "st", tv)
                            sse = ctx.do("s", lambda e, b=bank: e.activation(scr[:, 3072:3584], b[:], AF.Square), "ss", 1)
                            ctx.wait("v", "ss", sse)
                            ctx.do("v", lambda e: e.tensor_reduce(
                                scr2[:, 128:132], scr[:, 3072:3584].rearrange("p (h d) -> p h d", h=4), AX.X, ALU.add))
                            vv = ctx.do("v", lambda e: e.drain(), "sv", 1)
                            ctx.wait("s", "sv", vv)
                            sse = ctx.do("s", lambda e: e.activation(scr2[:, 128:132], scr2[:, 128:132], AF.Sqrt, bias=C(44)), "ss", 1)
                            ctx.wait("v", "ss", sse)
                            ctx.do("v", lambda e: e.reciprocal(scr2[:, 128:132], scr2[:, 128:132]))
                            for hh in range(4):
                                ctx.do("v", lambda e, b=bank, hh=hh, sA=sA: e.tensor_scalar(
                                    sA[:, 128 * hh:128 * hh + 128], b[:, 128 * hh:128 * hh + 128],
                                    scr2[:, 128 + hh:129 + hh], None, ALU.mult))
                        else:
                            ctx.wait("v", "st", tv)
                            ctx.do("v", lambda e, b=bank, sA=sA: e.tensor_copy(sA, b[:]))
                        x1 = h4(sA)[:, :, 0:64]
                        x2 = h4(sA)[:, :, 64:128]
                        o1 = h4(sB)[:, :, 0:64]
                        o2 = h4(sB)[:, :, 64:128]
                        t1 = h4(sC)[:, :, 0:64]
                        t2 = h4(sD)[:, :, 0:64]
                        ctx.do("v", lambda e, x1=x1, tt=tt, t1=t1: e.tensor_tensor(t1, x1, ropev(tt), ALU.mult))
                        ctx.do("v", lambda e, x2=x2, tt=tt, t2=t2: e.tensor_tensor(t2, x2, ropesv(tt), ALU.mult))
                        ctx.do("v", lambda e, t1=t1, t2=t2, o1=o1: e.tensor_tensor(o1, t1, t2, ALU.subtract))
                        ctx.do("v", lambda e, x1=x1, tt=tt, t1=t1: e.tensor_tensor(t1, x1, ropesv(tt), ALU.mult))
                        ctx.do("v", lambda e, x2=x2, tt=tt, t2=t2: e.tensor_tensor(t2, x2, ropev(tt), ALU.mult))
                        vv = ctx.do("v", lambda e, t1=t1, t2=t2, o2=o2: e.tensor_tensor(o2, t1, t2, ALU.add), "sv", 1)
                        evfree[tt % 2] = vv
                        ctx.wait("y", "sv", vv)
                        stq[tt % 2] = role_dma("y",
                                 [(target[0][128 * tt:128 * tt + 128, 512 * nsl:512 * nsl + 512], sB, False)],
                                 [(target[1][128 * tt:128 * tt + 128, 512 * nsl:512 * nsl + 512], sB, False)],
                                 sem=f"qs{tt % 2}")
            ctx.barrier()
        bqy = 0
        for tt in range(NT):
            if bqy:
                ctx.wait("v", "qb", bqy)
                ctx.wait("s", "qb", bqy)
            for k in range(KC):
                mme(pb[6][:, 0:16], fmA[:, k * TOK + 128 * tt:k * TOK + 128 * tt + 128],
                    c_wb[:, 16 * k:16 * k + 16], start=(k == 0), stop=(k == KC - 1))
            tv = tmark()
            ctx.wait("s", "st", tv)
            sse = ctx.do("s", lambda e: e.activation(scr2[:, 256:272], pb[6][:, 0:16], AF.Exp, scale=-1.0), "ss", 1)
            ctx.wait("v", "ss", sse)
            ctx.do("v", lambda e: e.tensor_scalar(scr2[:, 256:272], scr2[:, 256:272], 1.0, None, ALU.add))
            ctx.do("v", lambda e: e.reciprocal(scr2[:, 256:272], scr2[:, 256:272]))
            vv = ctx.do("v", lambda e: e.tensor_scalar(scr2[:, 272:288], scr2[:, 256:272], 2.0, None, ALU.mult), "sv", 1)
            ctx.wait("t", "sv", vv)
            vv2 = ctx.do("v", lambda e: e.tensor_copy(lnt[:, 0:16], scr2[:, 272:288]), "sv", 1)
            ctx.wait("t", "sv", vv2)
            tv = ctx.do("t", lambda e: e.transpose(pbt_b[0:16, 0:128], lnt[:, 0:16], c_id[:]), "st", 1)
            ctx.wait("v", "st", tv)
            vv3 = ctx.do("v", lambda e: e.tensor_copy(lnt[0:16, 256:384], pbt_b[0:16, 0:128]), "sv", 1)
            ctx.wait("y", "sv", vv3)
            bqy = role_dma("y",
                     [(shm_bt[0][128 * tt:128 * tt + 128, :], scr2[:, 272:288], False),
                      (shm_br[0][:, 128 * tt:128 * tt + 128], lnt[0:16, 256:384], False)],
                     [(shm_bt[1][128 * tt:128 * tt + 128, :], scr2[:, 272:288], False),
                      (shm_br[1][:, 128 * tt:128 * tt + 128], lnt[0:16, 256:384], False)],
                     sem="qb")
        ctx.barrier()
        ctx.wait("g", "qy")
        ctx.wait("g", "qs0")
        ctx.wait("g", "qs1")
        ctx.wait("g", "qb")
        sgp = pair_sync()
        attn_b_mix(sgp)
        phase()
        ctx.wait("g", "qy")
        ctx.wait("g", "qo")
        sg2n, sg2v = pair_sync()
        ctx.wait("y", sg2n, sg2v)
        ctx.wait("y", "qo")
        p0 = []
        p1 = []
        for s in range(2):
            for hl in range(8):
                chn = 8 * s + hl
                dst = fmA[:, chn * TOK:(chn + 1) * TOK]
                p0.append((dst, shm_o[s][hl][:, 0:TOK], False))
                p1.append((dst, shm_o[s][hl][:, TOK:2 * TOK], False))
        role_dma("y", p0, p1)
        ctx.wait("t", "qy")
        ctx.barrier()

        def evac_b(tt, nsl, bank):
            return ctx.do("v", lambda e, b=bank, tt=tt, nsl=nsl:
                          e.scalar_tensor_tensor(xres[:, tt * D + 512 * nsl:tt * D + 512 * nsl + 512],
                                                 b[:], 1.0, xres[:, tt * D + 512 * nsl:tt * D + 512 * nsl + 512],
                                                 ALU.mult, ALU.add), "sv", 1)
        proj_tok("bWo", fmA, evac_b)

    def attn_b_mix(sgp):
        ctx.wait("y", sgp[0], sgp[1])
        ctx.wait("y", "qs0")
        ctx.wait("y", "qs1")
        ctx.wait("y", "qb")
        for s in range(2):
            dma("y", betat[:, 128 * s:128 * (s + 1)].rearrange("p (c h) -> p c h", c=8),
                shm_bt[s].rearrange("(c p) h -> p c h", p=128), "qbt")
        ctx.wait("v", "qbt")
        POWP = [((2, 1, 0), (3, 0, 1)), ((4, 3, 2), (5, 2, 3)), ((6, 5, 4), (7, 4, 5)),
                ((8, 7, 6), (9, 6, 7)), ((10, 9, 8), (11, 8, 9)), (((12, 11, 10)),)]
        HORD = [12, 11, 9, 7, 5, 3, 1]

        def powslot(c, orig, w=128):
            return fmB[:, 8192 + 1664 * (c % 2) + 128 * orig:
                       8192 + 1664 * (c % 2) + 128 * orig + w]

        def XA(c):
            return fmB[:, 11648 + 512 * (c % 3):11648 + 512 * (c % 3) + 256]

        def XB(c):
            return fmB[:, 11648 + 512 * (c % 3) + 256:11648 + 512 * (c % 3) + 512]

        def A_ukt(c):
            return fmA[:, 14336 + 128 * c:14336 + 128 * c + 128]

        def emit_loads(i, batch):
            p0 = []
            p1 = []
            for s in range(2):
                for role_, plist, hg in ((0, p0, i), (1, p1, 8 + i)):
                    if batch == 2:
                        plist.append((A_qhT[:, 1024 * s:1024 * s + 1024],
                                      shm_q[s][:, 128 * hg:128 * hg + 128], True))
                        continue
                    plist.append((A_khT[:, 1024 * s:1024 * s + 1024],
                                  shm_k[s][:, 128 * hg:128 * hg + 128], True))
                    plist.append((A_ktok[:, 1024 * s:1024 * s + 1024].rearrange("p (c d) -> p c d", c=8),
                                  shm_k[s][:, 128 * hg:128 * hg + 128].rearrange("(c p) d -> p c d", p=128), False))
                    plist.append((A_vtok[:, 1024 * s:1024 * s + 1024].rearrange("p (c d) -> p c d", c=8),
                                  shm_v[s][:, 128 * hg:128 * hg + 128].rearrange("(c p) d -> p c d", p=128), False))
                    plist.append((betarow[0:1, 1024 * s:1024 * s + 1024],
                                  shm_br[s][hg:hg + 1, :], False))
            return role_dma("y", p0, p1)

        seq_tdone = [0]
        qv_next = [None]
        for i in range(NSEQ):
            if seq_tdone[0]:
                ctx.wait("v", "st", seq_tdone[0])
                ctx.wait("s", "st", seq_tdone[0])
            if qv_next[0] is None:
                if seq_tdone[0]:
                    ctx.wait("y", "st", seq_tdone[0])
                    ctx.wait("y", "sv")
                emit_loads(i, 1)
                qv = emit_loads(i, 2)
            else:
                qv = qv_next[0]
            svs = ctx.sem("sv")
            prevv = ctx.cnt["sv"]
            ctx.cnt["sv"] = prevv + 1

            def bc(e, i=i, prevv=prevv, svs=svs):
                if prevv:
                    e.wait_ge(svs, prevv)
                rr = ctx.regs["v_role"]
                bt3 = betat.rearrange("p (c h) -> p c h", h=16)
                dst3 = betacol.unsqueeze(2)
                with e.If_eq(rr, 0):
                    e.tensor_copy(dst3, bt3[:, :, i:i + 1]).then_inc(svs, 1)
                with e.Else():
                    e.tensor_copy(dst3, bt3[:, :, 8 + i:8 + i + 1]).then_inc(svs, 1)
            ctx.wait("v", "qy", qv)
            ctx.on("v", bc)
            ctx.wait("t", "qy", qv)
            ktTrdy = [0] * 4
            for g4 in range(4):
                tv = ctx.do("t", lambda e, g4=g4: e.matmul(pb[5][:], c_ones1x[:],
                                                           betarow[0:1, 512 * g4:512 * g4 + 512],
                                                           start=True, stop=True, skip_group_check=True), "st", 1)
                ctx.wait("v", "st", tv)
                vv = ctx.do("v", lambda e, g4=g4: e.tensor_tensor(
                    A_ktT[:, 512 * g4:512 * g4 + 512], A_khT[:, 512 * g4:512 * g4 + 512],
                    pb[5][:], ALU.mult), "sv", 1)
                ctx.wait("t", "sv", vv)
                ktTrdy[g4] = vv
            tsv = [0] * NCH
            for c in range(16):
                tsv[c] = ctx.do("v", lambda e, c=c: e.tensor_scalar(
                    A_kttok[:, 128 * c:128 * c + 128], A_ktok[:, 128 * c:128 * c + 128],
                    betacol[:, c:c + 1], None, ALU.mult), "sv", 1)
            ctx.do("v", lambda e: e.memset(S32, 0.0))
            msv = ctx.do("v", lambda e: e.memset(B_S16, 0.0), "sv", 1)

            aevac = [0] * NCH
            powrdy = {}
            bankfree = {2: ("ss", 0), 3: ("ss", 0)}
            chmm = [[0] * 7 for _ in range(NCH)]
            chtt = [[0] * 7 for _ in range(NCH)]
            xcp = [0] * NCH
            trst = [0] * NCH
            uktv = [0] * NCH
            m1st = [0] * NCH
            m2st = [0] * NCH
            vnsv = [0] * NCH
            sttv = [0] * NCH
            s32v = [0] * NCH
            expss = {}
            masksv = {}
            sbank = [0, 0]
            avfree = [0]
            avdone = [None]

            def emit_xcopy(c):
                if c >= 3:
                    ctx.wait("v", "st", chmm[c - 3][6])
                ctx.do("v", lambda e, c=c: e.tensor_copy(XA(c)[:, 0:128], A_vtok[:, 128 * c:128 * c + 128]))
                xcp[c] = ctx.do("v", lambda e, c=c: e.tensor_copy(XA(c)[:, 128:256], A_ktok[:, 128 * c:128 * c + 128]), "sv", 1)

            def emit_amm(c):
                if c >= 1:
                    ctx.wait("t", "sv", aevac[c - 1])
                ctx.wait("t", "sv", ktTrdy[c // 4])
                kh = A_khT[:, 128 * c:128 * c + 128]
                kt = A_ktT[:, 128 * c:128 * c + 128]
                mme(pb[0][:, 0:128], kh, kt, True, True)
                tv = mme(pb[0][:, 128:256], kt, kh, True, True, inc=True)
                ctx.wait("v", "st", tv)
                if c >= 2:
                    ctx.wait("v", "st", chmm[c - 2][6])
                vv = ctx.do("v", lambda e, c=c: e.tensor_tensor(
                    powslot(c, 0, 256), pb[0][:, 0:256], c_trilu[:], ALU.mult), "sv", 1)
                aevac[c] = vv
                powrdy[(c, 0)] = ("sv", vv)
                powrdy[(c, 1)] = ("sv", vv)

            def emit_pow(c, pp):
                pair = POWP[pp]
                bk = 2 + (pp % 2)
                fn, fv = bankfree[bk]
                ctx.wait("t", fn, fv)
                deps = set()
                for ent in pair:
                    deps.add(ent[1])
                    deps.add(ent[2])
                for o_ in sorted(deps):
                    n_, v_ = powrdy[(c, o_)]
                    ctx.wait("t", n_, v_)
                for ii, (dst, lh, rh) in enumerate(pair):
                    last = ii == len(pair) - 1
                    tv = mme(pb[bk][:, 128 * ii:128 * ii + 128],
                             powslot(c, lh), powslot(c, rh), True, True, inc=last)
                ctx.wait("s", "st", tv)
                if pp == 0 and c >= 2:
                    ctx.wait("s", "st", chmm[c - 2][6])
                w = 128 * len(pair)
                d0 = pair[0][0]
                sv_ = ctx.do("s", lambda e, c=c, d0=d0, bk=bk, w=w: e.activation(
                    powslot(c, d0, w), pb[bk][:, 0:w], AF.Identity), "ss", 1)
                for (dst, lh, rh) in pair:
                    powrdy[(c, dst)] = ("ss", sv_)
                bankfree[bk] = ("ss", sv_)

            def emit_chain(c, j):
                half = pb[1][:, 256 * (c % 2):256 * (c % 2) + 256]
                if j == 0:
                    ctx.wait("t", "sv", xcp[c])
                    if c >= 2:
                        ctx.wait("t", "sv", chtt[c - 2][6])
                else:
                    ctx.wait("t", "sv", chtt[c][j - 1])
                n_, v_ = powrdy[(c, HORD[j])]
                ctx.wait("t", n_, v_)
                xc = XA(c) if j % 2 == 0 else XB(c)
                xn = XB(c) if j % 2 == 0 else XA(c)
                if j == 6:
                    xn = XB(c)
                chmm[c][j] = mme(half, powslot(c, HORD[j]), xc, True, True, inc=True)
                ctx.wait("v", "st", chmm[c][j])
                op = ALU.subtract if j == 6 else ALU.add
                chtt[c][j] = ctx.do("v", lambda e, xn=xn, xc=xc, half=half, op=op:
                                    e.tensor_tensor(xn, xc, half, op), "sv", 1)

            def emit_transpose(c):
                ctx.wait("t", "sv", chtt[c][6])
                if c >= 8:
                    ctx.wait("t", "sv", uktv[c - 8])
                trst[c] = ctx.do("t", lambda e, c=c: e.transpose(
                    pbt_b[0:128, 128 * (c % 8):128 * (c % 8) + 128], XB(c)[:, 128:256], c_id[:]), "st", 1)
                ctx.wait("v", "st", trst[c])
                uktv[c] = ctx.do("v", lambda e, c=c: e.tensor_copy(
                    A_ukt(c), pbt_b[0:128, 128 * (c % 8):128 * (c % 8) + 128]), "sv", 1)

            def emit_seq(c):
                if _REG_BYPASS:
                    vnsv[c] = ctx.do("v", lambda e, c=c: e.tensor_copy(
                        A_vn[:, 128 * c:128 * c + 128], A_vtok[:, 128 * c:128 * c + 128]), "sv", 1)
                    return
                ctx.wait("t", "sv", uktv[c])
                ctx.wait("t", "sv", sttv[c - 1] if c >= 1 else msv)
                if c >= 1:
                    ctx.wait("t", "sv", vnsv[c - 1])
                m1st[c] = mme(pb[4][:, 0:128], A_ukt(c), B_S16, True, True, inc=True)
                ctx.wait("v", "st", m1st[c])
                vnsv[c] = ctx.do("v", lambda e, c=c: e.tensor_tensor(
                    A_vn[:, 128 * c:128 * c + 128], XB(c)[:, 0:128], pb[4][:, 0:128], ALU.subtract), "sv", 1)
                ctx.wait("t", "sv", vnsv[c])
                ctx.wait("t", "sv", tsv[c])
                if c >= 1:
                    ctx.wait("t", "sv", s32v[c - 1])
                m2st[c] = mme(pb[4][:, 128:256], A_kttok[:, 128 * c:128 * c + 128],
                              A_vn[:, 128 * c:128 * c + 128], True, True, inc=True)
                ctx.wait("v", "st", m2st[c])
                sttv[c] = ctx.do("v", lambda e: e.scalar_tensor_tensor(
                    B_S16, pb[4][:, 128:256], 1.0, S32, ALU.mult, ALU.add), "sv", 1)
                s32v[c] = ctx.do("v", lambda e: e.tensor_tensor(
                    S32, S32, pb[4][:, 128:256], ALU.add), "sv", 1)

            def emit_smm(qg, kt):
                bank = pb[5 + (kt % 2)]
                if sbank[kt % 2]:
                    ctx.wait("t", "ss", sbank[kt % 2])
                if avdone[0] is not None:
                    ctx.wait("t", "sv", avdone[0])
                tv = mme(bank[:], A_ktT[:, 128 * kt:128 * kt + 128],
                         A_qhT[:, 512 * qg:512 * qg + 512], True, True, inc=True)
                ctx.wait("s", "st", tv)
                sv_ = ctx.do("s", lambda e, kt=kt, b=bank: e.activation(
                    B_exp(kt), b[:], AF.Exp, scale=SCALE_B), "ss", 1)
                expss[kt] = sv_
                sbank[kt % 2] = sv_
                if kt >= 4 * qg:
                    o = kt - 4 * qg
                    ctx.wait("v", "ss", sv_)
                    if o > 0:
                        ctx.do("v", lambda e, kt=kt, o=o: e.memset(B_exp(kt)[:, 0:128 * o], 0.0))
                    masksv[kt] = ctx.do("v", lambda e, kt=kt, o=o: e.tensor_tensor(
                        B_exp(kt)[:, 128 * o:128 * o + 128], B_exp(kt)[:, 128 * o:128 * o + 128],
                        c_triu_incl[:], ALU.mult), "sv", 1)

            def emit_attn(qg):
                nkt = 4 * qg + 4
                fn2, fv2 = bankfree[2]
                ctx.wait("t", fn2, fv2)
                fn3, fv3 = bankfree[3]
                ctx.wait("t", fn3, fv3)
                if avfree[0]:
                    ctx.wait("t", "sv", avfree[0])
                for kt in range(nkt):
                    ctx.wait("t", "ss", expss[kt])
                    if kt in masksv:
                        ctx.wait("t", "sv", masksv[kt])
                    ctx.wait("t", "sv", vnsv[kt])
                    mme(pb[2][:], A_vn[:, 128 * kt:128 * kt + 128], B_exp(kt),
                        kt == 0, kt == nkt - 1)
                    mme(pb[3][0:1, :], c_onescol[:], B_exp(kt),
                        kt == 0, kt == nkt - 1)
                tv = tmark()
                ctx.wait("v", "st", tv)
                ctx.do("v", lambda e, qg=qg: e.tensor_copy(scr[:, 512 * qg:512 * qg + 512], pb[2][:]))
                vv = ctx.do("v", lambda e, qg=qg: e.tensor_copy(nrow[0:1, 512 * qg:512 * qg + 512], pb[3][0:1, :]), "sv", 1)
                avfree[0] = vv
                avdone[0] = vv
                bankfree[2] = ("sv", vv)
                bankfree[3] = ("sv", vv)
                masksv.clear()

            if _REG_BYPASS:
                for c in range(NCH):
                    emit_seq(c)
                for qg in range(4):
                    nkt = 4 * qg + 4
                    for kt in range(nkt):
                        emit_smm(qg, kt)
                    emit_attn(qg)
            else:
                smm_sched = {}
                for qg in range(4):
                    kts = list(range(4 * qg + 4))
                    st0 = 4 * qg + 2 if qg else 1
                    nstg = 2 if qg == 3 else 4
                    for idx, kt in enumerate(kts):
                        smm_sched.setdefault(st0 + idx % nstg, []).append((qg, kt))
                for k in range(18):
                    if k <= 15:
                        emit_xcopy(k)
                    smms = list(smm_sched.get(k, []))

                    def pop_smm():
                        if smms:
                            qg_, kt_ = smms.pop(0)
                            emit_smm(qg_, kt_)
                    if k == 0:
                        emit_amm(0)
                        for pp in range(6):
                            emit_pow(0, pp)
                    elif k <= 16:
                        c = k - 1
                        pows = ([("a",)] + [("p", pp) for pp in range(6)]) if k <= 15 else []
                        wseq = []
                        for j in range(7):
                            wseq.append(("c", j))
                            wseq.append(None)
                            wseq.append(None)
                        pi2 = 0
                        for slot_i, s_ in enumerate(wseq):
                            if s_ is None:
                                if pi2 < len(pows):
                                    it = pows[pi2]
                                    pi2 += 1
                                    if it[0] == "a":
                                        emit_amm(k)
                                    else:
                                        emit_pow(k, it[1])
                                else:
                                    pop_smm()
                            elif s_[0] == "c":
                                emit_chain(c, s_[1])
                                pop_smm()
                        while pi2 < len(pows):
                            it = pows[pi2]
                            pi2 += 1
                            if it[0] == "a":
                                emit_amm(k)
                            else:
                                emit_pow(k, it[1])
                    while smms:
                        pop_smm()
                    if 1 <= k <= 16:
                        emit_transpose(k - 1)
                    if k == 16 and i + 1 < NSEQ:
                        ctx.wait("y", "sv", aevac[15])
                        ctx.wait("y", "sv", xcp[15])
                        emit_loads(i + 1, 1)
                        ctx.wait("y", "ss", expss[15])
                        qv_next[0] = emit_loads(i + 1, 2)
                    if 2 <= k <= 17:
                        emit_seq(k - 2)
                    if k in (5, 9, 13, 17):
                        emit_attn((k - 5) // 4)
            ctx.wait("s", "sv", avfree[0])
            ctx.do("s", lambda e: e.activation(nrow, nrow, AF.Ln))
            sse = ctx.do("s", lambda e: e.activation(nrow, nrow, AF.Exp, scale=-1.0), "ss", 1)
            ctx.wait("v", "ss", sse)
            vv = ctx.do("v", lambda e: e.tensor_copy(nrowb, nrow), "sv", 1)
            ctx.wait("t", "sv", vv)
            ctx.wait("t", "ss", sbank[0])
            ctx.wait("t", "ss", sbank[1])
            for qg in range(4):
                tv = ctx.do("t", lambda e, qg=qg: e.matmul(pb[5][:], c_ones1x[:],
                                                           nrowb[0:1, 512 * qg:512 * qg + 512],
                                                           start=True, stop=True, skip_group_check=True), "st", 1)
                ctx.wait("v", "st", tv)
                vv = ctx.do("v", lambda e, qg=qg: e.tensor_tensor(
                    B_oT[:, 512 * qg:512 * qg + 512], scr[:, 512 * qg:512 * qg + 512],
                    pb[5][:], ALU.mult), "sv", 1)
                ctx.wait("t", "sv", vv)
            ctx.wait("y", "sv")
            role_dma("y", [(shm_o[0][i], B_oT, False)], [(shm_o[1][i], B_oT, False)], sem="qo")
            ctx.wait("v", "qo")
            seq_tdone[0] = tmark()

    pbt_b = stack.enter_context(nc.psum_tensor("pbt_b", [128, 1024], BF16))

    try:
        attn_a()
        if nsb >= 2:
            wq_prep()
            ffn()
        if nsb >= 3:
            attn_b()
        if nsb >= 4:
            ffn()
    except _StopEmit:
        ctx.barrier()

    ctx.wait("y", "sv")
    ctx.wait("y", "st")
    ctx.wait("y", "ss")
    for i in range(NT):
        dma("y", y_out[128 * i:128 * (i + 1), :], xres[:, i * D:(i + 1) * D], "qy")
    ctx.wait("y", "qy")

    RT = 32 if sim else 16
    with nc.Block() as block:
        @block.sync
        def _(e):
            e.dma_start(out=srole[:], in_=dr["role"][:]).then_inc(ctx.sem("qy_role"), 16)
            if sim:
                e.dma_start(out=sppnc[:], in_=dr["ppnc"][:]).then_inc(ctx.sem("qy_role"), 16)
            e.wait_ge(ctx.sem("qy_role"), RT)
            r = e.alloc_register()
            e.load(r, srole[0:1, 0:1])
            ctx.regs["y_role"] = r
            for f in ctx.ops["y"]:
                f(e)

        @block.gpsimd
        def _(e):
            if sim:
                e.load_library(library_config.remote_dma)
            e.wait_ge(ctx.sem("qy_role"), RT)
            r = e.alloc_register()
            e.load(r, srole[0:1, 0:1])
            ctx.regs["g_role"] = r
            if sim:
                pp = e.alloc_register()
                e.load(pp, sppnc[0:1, 0:1])
                ctx.regs["g_ppnc"] = pp
            for f in ctx.ops["g"]:
                f(e)

        @block.tensor
        def _(e):
            for f in ctx.ops["t"]:
                f(e)

        @block.vector
        def _(e):
            e.wait_ge(ctx.sem("qy_role"), RT)
            r = e.alloc_register()
            e.load(r, srole[0:1, 0:1])
            ctx.regs["v_role"] = r
            for f in ctx.ops["v"]:
                f(e)

        @block.scalar
        def _(e):
            for f in ctx.ops["s"]:
                f(e)

    stack.close()
    return nc



def _host_consts(role, core=None):
    if core is None:
        core = role
    c = {}
    my_pnc = node_relative_to_routing_address(core % 8, 8)[1]
    c["ppnc"] = np.array([[my_pnc ^ 1]], dtype=np.int32)
    c["ident"] = np.eye(128, dtype=BF)
    i = np.arange(128)
    c["trilmask"] = (i[:, None] > i[None, :]).astype(BF)
    c["triumask"] = (i[:, None] < i[None, :]).astype(BF)
    c["triuincl"] = (i[:, None] <= i[None, :]).astype(BF)
    c["trilu"] = np.concatenate([(i[:, None] > i[None, :]).astype(BF),
                                 (i[:, None] < i[None, :]).astype(BF)], axis=1)
    es = np.zeros((32, KC * 128), dtype=np.float32)
    for ch in range(KC):
        es[2 * ch, 128 * ch:128 * ch + 64] = 1.0
        es[2 * ch + 1, 128 * ch + 64:128 * ch + 128] = 1.0
    c["esel"] = es.astype(BF)
    c["ones1x"] = np.ones((1, 128), dtype=BF)
    c["onescol"] = np.ones((128, 1), dtype=BF)
    pos = (TOK * role + np.arange(TOK)).astype(np.float32)
    inv = (1.0 / (10000.0 ** (np.arange(0, 128, 2, dtype=np.float32) / np.float32(128.0)))).astype(np.float32)
    th = pos[:, None] * inv[None, :]
    c["ropec"] = np.cos(th).astype(np.float32).reshape(NT, 128, 64).transpose(1, 0, 2).reshape(128, NT * 64).astype(BF)
    c["ropes"] = np.sin(th).astype(np.float32).reshape(NT, 128, 64).transpose(1, 0, 2).reshape(128, NT * 64).astype(BF)
    c["role"] = np.array([[role]], dtype=np.int32)
    return c



def kernel(L0_attn_a_hidden_states, L0_attn_a_Wq, L0_attn_a_Wk, L0_attn_a_Wv,
           L0_attn_a_Wo, L0_attn_a_bo, L0_attn_b_Wq, L0_attn_b_Wk, L0_attn_b_Wv,
           L0_attn_b_Wo, L0_attn_b_Wb, L0_ffn_weight):
    x = np.ascontiguousarray(np.asarray(L0_attn_a_hidden_states, dtype=np.float32))
    weights = {
        "aWq": np.ascontiguousarray(np.asarray(L0_attn_a_Wq, dtype=np.float32)),
        "aWk": np.ascontiguousarray(np.asarray(L0_attn_a_Wk, dtype=np.float32)),
        "aWv": np.ascontiguousarray(np.asarray(L0_attn_a_Wv, dtype=np.float32)),
        "aWo": np.ascontiguousarray(np.asarray(L0_attn_a_Wo, dtype=np.float32)),
        "bWq": np.ascontiguousarray(np.asarray(L0_attn_b_Wq, dtype=np.float32)),
        "bWk": np.ascontiguousarray(np.asarray(L0_attn_b_Wk, dtype=np.float32)),
        "bWv": np.ascontiguousarray(np.asarray(L0_attn_b_Wv, dtype=np.float32)),
        "bWo": np.ascontiguousarray(np.asarray(L0_attn_b_Wo, dtype=np.float32)),
        "fW": np.ascontiguousarray(np.asarray(L0_ffn_weight, dtype=np.float32)),
        "bWb": np.ascontiguousarray(np.asarray(L0_attn_b_Wb, dtype=np.float32)),
        "abo": np.ascontiguousarray(np.asarray(L0_attn_a_bo, dtype=np.float32))[None, :],
    }
    key = (_SUBBLOCKS, _PHASES, _is_sim())
    if key not in _CACHE:
        _CACHE[key] = build_program(_SUBBLOCKS)
    nc = _CACHE[key]
    in_maps = []
    for c in range(8):
        b, r = c // 2, c % 2
        m = dict(weights)
        m["x"] = np.ascontiguousarray(x[b, TOK * r:TOK * (r + 1), :])
        m.update(_host_consts(r, core=c))
        in_maps.append(m)
    res = run_bass_kernel_spmd(nc, in_maps, core_ids=list(range(8)))
    out = np.empty((4, 2048, D), dtype=np.float32)
    for c in range(8):
        b, r = c // 2, c % 2
        out[b, TOK * r:TOK * (r + 1), :] = res.results[c]["y"]
    return out


# version 09
# speedup vs baseline: 2.6862x; 2.6862x over previous
import os
import numpy as np
import ml_dtypes
from contextlib import ExitStack

import concourse.bass as bass
import concourse.mybir as mybir
from concourse import library_config
from concourse.libnrt import node_relative_to_routing_address
from concourse.bass_utils import run_bass_kernel_spmd

F32 = mybir.dt.float32
BF16 = mybir.dt.bfloat16
I32 = mybir.dt.int32
AF = mybir.ActivationFunctionType
ALU = mybir.AluOpType
AX = mybir.AxisListType
BF = ml_dtypes.bfloat16

D = 2048
TOK = 1024
NT = 8
KC = 16
MAGIC = 0x3E50000
SCALE_B = float(np.float32(0.08838834764831843))

NSEQ = 8
NCH = 16
_SUBBLOCKS = 4
_PHASES = 999
_LN_PLAIN = False
_REG_BYPASS = False
_FORCE_SIM_VARIANT = False
_CACHE = {}


class _StopEmit(Exception):
    pass


def _is_sim():
    return os.environ.get("BASS_SIM_FP64") == "1" or _FORCE_SIM_VARIANT


class Ctx:
    ENGS = ("t", "v", "s", "y", "g")

    def __init__(self, nc, stack):
        self.nc = nc
        self.stack = stack
        self.ops = {k: [] for k in self.ENGS}
        self._sems = {}
        self.cnt = {}
        self.regs = {}

    def sem(self, name):
        if name not in self._sems:
            self._sems[name] = self.stack.enter_context(self.nc.semaphore(name=name))
            self.cnt[name] = 0
        return self._sems[name]

    def on(self, eng, fn):
        self.ops[eng].append(fn)

    def wait(self, eng, name, v=None):
        s = self.sem(name)
        v = self.cnt[name] if v is None else v
        if v > 0:
            self.ops[eng].append(lambda e, s=s, v=v: e.wait_ge(s, v))

    SELF_SEM = {"v": "sv", "s": "ss"}

    def do(self, eng, fn, sem=None, n=1):
        if eng in self.SELF_SEM:
            name = self.SELF_SEM[eng]
            assert sem in (None, name), (eng, sem)
            s = self.sem(name)
            prev = self.cnt[name]
            if prev:
                self.ops[eng].append(lambda e, s=s, v=prev: e.wait_ge(s, v))
            self.cnt[name] = prev + 1
            self.ops[eng].append(lambda e, s=s, fn=fn: fn(e).then_inc(s, 1))
            return self.cnt[name]
        if sem is None:
            self.ops[eng].append(fn)
            return None
        s = self.sem(sem)
        self.cnt[sem] += n
        self.ops[eng].append(lambda e, s=s, n=n, fn=fn: fn(e).then_inc(s, n))
        return self.cnt[sem]

    def val(self, name):
        self.sem(name)
        return self.cnt[name]

    def barrier(self):
        vals = {}
        for en in self.ENGS:
            if en == "y":
                vals[en] = self.do(en, lambda e: e.nop(), "by", 1)
            elif en in self.SELF_SEM:
                vals[en] = self.do(en, lambda e: e.drain())
            else:
                vals[en] = self.do(en, lambda e: e.drain(), "b" + en, 1)
        for en in self.ENGS:
            for other in self.ENGS:
                name = {"y": "by", "v": "sv", "s": "ss"}.get(other, "b" + other)
                if other != en:
                    self.wait(en, name, vals[other])


def build_program(nsb, phases=None):
    nph = _PHASES if phases is None else phases
    nc = bass.Bass("TRN2", target_bir_lowering=False)
    stack = ExitStack()
    sim = _is_sim()

    dr = {}

    def din(name, shape, dt=F32):
        dr[name] = nc.dram_tensor(name, list(shape), dt, kind="ExternalInput").ap()

    din("x", (TOK, D))
    for w in ("aWq", "aWk", "aWv", "aWo", "bWq", "bWk", "bWv", "bWo", "fW"):
        din(w, (D, D))
    din("bWb", (D, 16))
    din("abo", (1, D))
    din("ident", (128, 128), BF16)
    din("trilmask", (128, 128), BF16)
    din("triumask", (128, 128), BF16)
    din("triuincl", (128, 128), BF16)
    din("trilu", (128, 256), BF16)
    din("esel", (32, KC * 128), BF16)
    din("ones1x", (1, 128), BF16)
    din("onescol", (128, 1), BF16)
    din("ropec", (128, NT * 64), BF16)
    din("ropes", (128, NT * 64), BF16)
    din("role", (1, 1), I32)
    din("ppnc", (1, 1), I32)
    y_out = nc.dram_tensor("y", [TOK, D], F32, kind="ExternalOutput").ap()

    ln_hbm = nc.dram_tensor("ln_hbm", [TOK, D], BF16, kind="Internal").ap()
    wq_hbm = nc.dram_tensor("wq_hbm", [D, D], BF16, kind="Internal").ap()
    shm_kv = nc.dram_tensor("shm_kv", [2, 128, 1040], F32, kind="Internal", addr_space="Shared").ap()
    shm_q = nc.dram_tensor("shm_q", [2, TOK, D], BF16, kind="Internal", addr_space="Shared").ap()
    shm_k = nc.dram_tensor("shm_k", [2, TOK, D], BF16, kind="Internal", addr_space="Shared").ap()
    shm_v = nc.dram_tensor("shm_v", [2, TOK, D], BF16, kind="Internal", addr_space="Shared").ap()
    shm_br = nc.dram_tensor("shm_br", [2, 16, TOK], BF16, kind="Internal", addr_space="Shared").ap()
    shm_bt = nc.dram_tensor("shm_bt", [2, TOK, 16], F32, kind="Internal", addr_space="Shared").ap()
    shm_o = nc.dram_tensor("shm_o", [2, 8, 128, 2 * TOK], BF16, kind="Internal", addr_space="Shared").ap()
    shm_fl = nc.dram_tensor("shm_fl", [2, 16], I32, kind="Internal", addr_space="Shared").ap()

    sb = {}

    def sbuf(name, shape, dt):
        sb[name] = stack.enter_context(nc.sbuf_tensor(name, list(shape), dt))
        return sb[name]

    xres = sbuf("xres", (128, NT * D), F32)
    fmA = sbuf("fmA", (128, KC * TOK), BF16)
    fmB = sbuf("fmB", (128, KC * TOK), BF16)
    wbuf = sbuf("wbuf", (128, 2 * KC * 512), BF16)
    scr = sbuf("scr", (128, 4096), F32)
    scr2 = sbuf("scr2", (128, 1024), F32)
    cols = sbuf("cols", (128, 64), F32)
    lnt = sbuf("lnt", (128, 2 * D), BF16)
    f32ones = sbuf("f32ones", (1, 128), F32)
    f32onescol = sbuf("f32onescol", (128, 1), F32)
    inva = sbuf("inva", (1, 1), F32)
    kvf = sbuf("kvf", (128, 1024), BF16)
    ksumS = sbuf("ksumS", (128, KC * 32), BF16)
    nrmvb = sbuf("nrmvb", (32, 1024), BF16)
    c_id = sbuf("c_id", (128, 128), BF16)
    c_tril = sbuf("c_tril", (128, 128), BF16)
    c_triu = sbuf("c_triu", (128, 128), BF16)
    c_triu_incl = sbuf("c_triu_incl", (128, 128), BF16)
    c_trilu = sbuf("c_trilu", (128, 256), BF16)
    c_esel = sbuf("c_esel", (32, KC * 128), BF16)
    c_ones1x = sbuf("c_ones1x", (1, 128), BF16)
    c_onescol = sbuf("c_onescol", (128, 1), BF16)
    c_ropec = sbuf("c_ropec", (128, NT * 64), BF16)
    c_ropes = sbuf("c_ropes", (128, NT * 64), BF16)
    c_bo = sbuf("c_bo", (1, D), BF16)
    c_wb = sbuf("c_wb", (128, KC * 16), BF16)
    srole = sbuf("srole", (1, 1), I32)
    sflag = sbuf("sflag", (1, 16), I32)
    sppnc = sbuf("sppnc", (1, 1), I32)
    sdtx = sbuf("sdtx", (128, 1), F32)
    sdrx = sbuf("sdrx", (128, 1), F32)

    kvb = scr[:, 0:1040]
    betat = scr2[:, 128:384]
    betacol = scr2[:, 384:400]
    kvp = scr[:, 1040:2080]
    nrmv = scr2[0:32, 0:1024]
    nrow = scr[0:1, 2048:4096]
    nrowb = lnt[0:1, 2048:4096]
    betarow = lnt[0:1, 0:2048]

    pb = [stack.enter_context(nc.psum_tensor(f"pb{_i}", [128, 512], F32)) for _i in range(7)]

    ctx = Ctx(nc, stack)
    phc = [0]

    def phase():
        phc[0] += 1
        if phc[0] >= nph:
            raise _StopEmit()
    if sim:
        pass

    A_qhT = fmA[:, 0:2048]
    A_khT = fmA[:, 2048:4096]
    A_ktT = fmA[:, 4096:6144]
    A_ktok = fmA[:, 6144:8192]
    A_kttok = fmA[:, 8192:10240]
    A_vtok = fmA[:, 10240:12288]
    A_vn = fmA[:, 12288:14336]
    REGOFF = 8192

    def B_exp(kt):
        return fmB[:, 512 * kt:512 * kt + 512]

    def B_reg(j):
        return fmB[:, REGOFF + 128 * j:REGOFF + 128 * j + 128]

    def B_X(p):
        return fmB[:, REGOFF + 13 * 128 + 128 * p:REGOFF + 13 * 128 + 128 * p + 128]

    B_S16 = fmB[:, 13184:13312]
    B_oT = fmB[:, 13312:15360]
    S32 = scr2[:, 0:128]
    RMAGIC = 12582912.0

    def C(j):
        return cols[:, j:j + 1]

    def dma(eng, out, in_, sem="qy", transpose=False):
        if transpose:
            return ctx.do(eng, lambda e, o=out, i=in_: e.dma_start(out=o, in_=i, transpose=True), sem, 16)
        return ctx.do(eng, lambda e, o=out, i=in_: e.dma_start(out=o, in_=i), sem, 16)

    def role_dma(eng, p0, p1, sem="qy"):
        assert len(p0) == len(p1)
        s = ctx.sem(sem)
        ctx.cnt[sem] += 16 * len(p0)

        def f(e, p0=p0, p1=p1, s=s, eng=eng):
            rr = ctx.regs[eng + "_role"]
            with e.If_eq(rr, 0):
                for (o, i, tr) in p0:
                    ins = e.dma_start(out=o, in_=i, transpose=True) if tr else e.dma_start(out=o, in_=i)
                    ins.then_inc(s, 16)
            with e.Else():
                for (o, i, tr) in p1:
                    ins = e.dma_start(out=o, in_=i, transpose=True) if tr else e.dma_start(out=o, in_=i)
                    ins.then_inc(s, 16)
            return None
        ctx.on(eng, f)
        return ctx.cnt[sem]

    sync_round = [0]

    def pair_sync():
        sync_round[0] += 1
        rnd = sync_round[0]
        if sim:
            rs = ctx.sem(f"rsem{rnd}")
            ls = ctx.sem(f"lsem{rnd}")
            pp = ctx.sem(f"prp{rnd}")
            def f(e, rs=rs, ls=ls, pp=pp):
                p = ctx.regs["g_ppnc"]
                e.remote_dma(sdrx[:], sdtx[:], rs, ls, p, 0, 0x10).then_inc(pp, 1)
                e.wait_ge(pp, 1)
                e.trigger_dma()
                return e.wait_ge(ls, 1)
            ctx.do("g", f, "sg", 1)
            return (f"rsem{rnd}", 1)
        vv = ctx.do("v", lambda e, r=rnd: e.memset(sflag[:], MAGIC + r), "sv", 1)
        ctx.wait("g", "sv", vv)
        qgf = ctx.sem("qgf")
        ctx.cnt["qgf"] += 16
        tgt = ctx.cnt["qgf"]

        def f(e, rnd=rnd, tgt=tgt):
            rr = ctx.regs["g_role"]
            with e.If_eq(rr, 0):
                e.dma_start(out=shm_fl[0:1, :], in_=sflag[:]).then_inc(qgf, 16)
            with e.Else():
                e.dma_start(out=shm_fl[1:2, :], in_=sflag[:]).then_inc(qgf, 16)
            e.wait_ge(qgf, tgt)
            f1 = e.alloc_register()
            nd = e.alloc_register()
            with e.If_eq(rr, 0):
                e.load(f1, shm_fl[1:2, 0:1])
                e.reg_alu(nd, f1, MAGIC + rnd, ALU.not_equal)
                with e.While(nd):
                    e.load(f1, shm_fl[1:2, 0:1])
                    e.reg_alu(nd, f1, MAGIC + rnd, ALU.not_equal)
            with e.Else():
                e.load(f1, shm_fl[0:1, 0:1])
                e.reg_alu(nd, f1, MAGIC + rnd, ALU.not_equal)
                with e.While(nd):
                    e.load(f1, shm_fl[0:1, 0:1])
                    e.reg_alu(nd, f1, MAGIC + rnd, ALU.not_equal)
            return e.nop()
        ctx.do("g", f, "sg", 1)
        return ("sg", ctx.val("sg"))

    def load_wcb(wname, nsl, d):
        src = dr[wname][:, 512 * nsl:512 * (nsl + 1)].rearrange("(k p) f -> p k f", p=128)
        dst = wbuf[:, d * 8192:(d + 1) * 8192].rearrange("p (k f) -> p k f", k=KC)
        return dma("g", dst, src, f"qg{d}")

    def wslice(d, k, lo=0, width=512):
        return wbuf[:, d * 8192 + k * 512 + lo:d * 8192 + k * 512 + lo + width]

    def mme(bank_ap, lhsT, rhs, start, stop, inc=False):
        fn = lambda e, b=bank_ap, l=lhsT, r=rhs, st=start, sp=stop: \
            e.matmul(b, l, r, start=st, stop=sp, skip_group_check=True)
        if inc:
            return ctx.do("t", fn, "st", 1)
        ctx.do("t", fn)
        return None

    def tmark():
        s = ctx.sem("st")
        ctx.cnt["st"] += 1
        last = ctx.ops["t"].pop()
        ctx.ops["t"].append(lambda e, f=last, s=s: f(e).then_inc(s, 1))
        return ctx.cnt["st"]

    ctx.sem("qy_role")
    ctx.cnt["qy_role"] = 32 if sim else 16
    ctx.do("v", lambda e: e.memset(C(43), 1e-5))
    ctx.do("v", lambda e: e.memset(C(45), RMAGIC))
    ctx.do("v", lambda e: e.memset(C(46), -RMAGIC))
    if sim:
        ctx.do("v", lambda e: e.memset(sdtx[:], 1.0))
    ctx.do("v", lambda e: e.memset(C(44), 1e-6), "sv", 1)
    for i in range(NT):
        dma("y", xres[:, i * D:(i + 1) * D], dr["x"][128 * i:128 * (i + 1), :], "qy")
    for nm, t in (("ident", c_id), ("trilmask", c_tril), ("triumask", c_triu), ("triuincl", c_triu_incl),
                  ("trilu", c_trilu),
                  ("esel", c_esel), ("ones1x", c_ones1x), ("onescol", c_onescol),
                  ("ropec", c_ropec), ("ropes", c_ropes)):
        dma("y", t[:], dr[nm][:], "qy")
    dma("g", c_bo[:], dr["abo"][:], "qg")
    dma("g", c_wb[:].rearrange("p (k h) -> p k h", k=KC),
        dr["bWb"][:].rearrange("(k p) h -> p k h", p=128), "qg")
    for en in ("t", "v", "s", "g"):
        ctx.wait(en, "qy")
        ctx.wait(en, "qg")
    ctx.barrier()


    def ln_stage(ffn_mode):
        ffn_mode = ffn_mode and not _LN_PLAIN
        ctx.wait("s", "qy")
        ctx.wait("v", "qy")
        for _en in ("s", "v"):
            ctx.wait(_en, "qp0")
            ctx.wait(_en, "qp1")
        prevts = [0]
        for i in range(NT):
            xt = xres[:, i * D:(i + 1) * D]
            if prevts[0]:
                ctx.wait("s", "sv", prevts[0])
            sqv = ctx.do("s", lambda e, xt=xt: e.activation(scr[:, 0:2048], xt, AF.Square,
                                                            accum_out=C(36)), "ss", 1)
            ctx.do("v", lambda e, xt=xt: e.tensor_reduce(C(35), xt, AX.X, ALU.add))
            ctx.wait("v", "ss", sqv)
            ctx.do("v", lambda e: e.tensor_scalar(C(32), C(35), 1.0 / D, None, ALU.mult))
            ctx.do("v", lambda e: e.tensor_tensor(C(33), C(32), C(32), ALU.mult))
            vv = ctx.do("v", lambda e: e.scalar_tensor_tensor(C(34), C(36), 1.0 / D, C(33), ALU.mult, ALU.subtract), "sv", 1)
            ctx.wait("s", "sv", vv)
            sv2 = ctx.do("s", lambda e: e.activation(C(34), C(34), AF.Sqrt, bias=C(43)), "ss", 1)
            ctx.wait("v", "ss", sv2)
            ctx.do("v", lambda e, i=i: e.reciprocal(C(i), C(34)))
            vv = ctx.do("v", lambda e, i=i: e.scalar_tensor_tensor(C(8 + i), C(32), -1.0, C(i), ALU.mult, ALU.mult), "sv", 1)
            half = (i % 2) * D
            if not ffn_mode:
                ctx.wait("s", "sv", vv)
                if i >= 2:
                    ctx.wait("s", f"ql{i % 2}", stv[i % 2])
                ssv = ctx.do("s", lambda e, xt=xt, i=i, half=half: e.activation(
                    lnt[:, half:half + D], xt, AF.Identity, bias=C(8 + i), scale=C(i)), "ss", 1)
                ctx.wait("y", "ss", ssv)
                stv[i % 2] = dma("y", ln_hbm[128 * i:128 * (i + 1), :], lnt[:, half:half + D], f"ql{i % 2}")
            else:
                ctx.wait("s", "sv", vv)
                uv = scr[:, 2048:4096]
                ssv = ctx.do("s", lambda e, xt=xt, uv=uv, i=i: e.activation(
                    uv, xt, AF.Identity, bias=C(8 + i), scale=C(i)), "ss", 1)
                ctx.wait("v", "ss", ssv)
                ctx.do("v", lambda e, uv=uv: e.tensor_reduce(C(37), uv, AX.X, ALU.max,
                                                             apply_absolute_value=True))
                ctx.do("v", lambda e, i=i: e.tensor_scalar(C(16 + i), C(37), 1.0 / 127.0, 1e-8, ALU.mult, ALU.max))
                ctx.do("v", lambda e, i=i: e.tensor_tensor(C(24 + i), C(16 + i), C(41), ALU.mult))
                ctx.do("v", lambda e, i=i: e.reciprocal(C(38), C(16 + i)))
                ctx.do("v", lambda e, uv=uv: e.tensor_scalar(
                    scr[:, 0:2048], uv, C(38), RMAGIC, ALU.mult, ALU.add))
                if i >= 2:
                    ctx.wait("v", f"ql{i % 2}", stv[i % 2])
                vv = ctx.do("v", lambda e, half=half: e.tensor_scalar(
                    lnt[:, half:half + D], scr[:, 0:2048], -RMAGIC, None, ALU.add), "sv", 1)
                prevts[0] = vv
                ctx.wait("y", "sv", vv)
                stv[i % 2] = dma("y", ln_hbm[128 * i:128 * (i + 1), :], lnt[:, half:half + D], f"ql{i % 2}")
        phase()
        ctx.wait("y", "qy")
        ctx.wait("y", "ql0")
        ctx.wait("y", "ql1")
        for k in range(KC):
            dma("y", fmA[:, k * TOK:(k + 1) * TOK], ln_hbm[:, 128 * k:128 * (k + 1)], "qy", transpose=True)
        ctx.wait("t", "qy")
        ctx.wait("v", "qy")
        ctx.wait("s", "qy")
        ctx.barrier()
        phase()

    stv = [0, 0]

    def proj_tok(wname, lhsT_arena, evac, extra_mm=None, wq_transpose=False):
        wpend = [0, 0]
        evfree = [0, 0]
        if wq_transpose:
            ctx.wait("y", "qp0")
            ctx.wait("y", "qp1")
        for nsl in range(4):
            d = nsl % 2
            weng = "y" if wq_transpose else "g"
            if nsl >= 2:
                ctx.wait(weng, "st", wpend[d])
            if wq_transpose:
                for k in range(KC):
                    src = wq_hbm[512 * nsl:512 * (nsl + 1), 128 * k:128 * (k + 1)]
                    qv = dma("y", wslice(d, k), src, f"qw{d}", transpose=True)
                ctx.wait("t", f"qw{d}", qv)
            else:
                qv = load_wcb(wname, nsl, d)
                ctx.wait("t", f"qg{d}", qv)
            for tt in range(NT):
                bank = pb[tt % 2]
                if evfree[tt % 2]:
                    ctx.wait("t", "sv", evfree[tt % 2])
                first = True
                if extra_mm is not None:
                    ctx.do("t", lambda e, b=bank, nsl=nsl: extra_mm(e, b, nsl))
                    first = False
                for k in range(KC):
                    mme(bank[:], lhsT_arena[:, k * TOK + 128 * tt:k * TOK + 128 * tt + 128],
                        wslice(d, k), start=(first and k == 0), stop=(k == KC - 1))
                tv = tmark()
                wpend[d] = tv
                ctx.wait("v", "st", tv)
                evfree[tt % 2] = evac(tt, nsl, bank)
        ctx.barrier()

    def attn_a():
        phase()
        ln_stage(False)
        wpend = [0, 0]
        evfree = [0, 0]
        for nsl in range(4):
            d = nsl % 2
            if nsl >= 2:
                ctx.wait("g", "st", wpend[d])
            qv = load_wcb("aWk", nsl, d)
            ctx.wait("t", f"qg{d}", qv)
            for tt in range(NT):
                bank = pb[tt % 2]
                if evfree[tt % 2]:
                    ctx.wait("t", "ss", evfree[tt % 2])
                for k in range(KC):
                    mme(bank[:], fmA[:, k * TOK + 128 * tt:k * TOK + 128 * tt + 128],
                        wslice(d, k), start=(k == 0), stop=(k == KC - 1))
                tv = tmark()
                wpend[d] = tv
                ctx.wait("s", "st", tv)
                evfree[tt % 2] = ctx.do("s", lambda e, b=bank, tt=tt, nsl=nsl:
                                        e.activation(fmB[:, tt * D + 512 * nsl:tt * D + 512 * nsl + 512],
                                                     b[:], AF.Relu), "ss", 1)
        ctx.barrier()
        phase()
        wpend = [0, 0]
        evfree = [0, 0]
        vsl = scr2[:, 512:1024].bitcast(BF16)
        ctx.do("v", lambda e: e.memset(kvb, 0.0), "sv", 1)
        kvfree = 0
        for nsl in range(4):
            d = nsl % 2
            if nsl >= 2:
                ctx.wait("g", "st", wpend[d])
            qv = load_wcb("aWv", nsl, d)
            ctx.wait("t", f"qg{d}", qv)
            for tt in range(NT):
                bank = pb[tt % 2]
                if evfree[tt % 2]:
                    ctx.wait("t", "sv", evfree[tt % 2])
                for k in range(KC):
                    mme(bank[:], fmA[:, k * TOK + 128 * tt:k * TOK + 128 * tt + 128],
                        wslice(d, k), start=(k == 0), stop=(k == KC - 1))
                tv = tmark()
                ctx.wait("v", "st", tv)
                vofs = (tt % 2) * 512
                evfree[tt % 2] = ctx.do("v", lambda e, b=bank, vofs=vofs:
                                        e.tensor_copy(vsl[:, vofs:vofs + 512], b[:]), "sv", 1)
                ctx.wait("t", "sv", evfree[tt % 2])
                if kvfree:
                    ctx.wait("t", "sv", kvfree)
                for j in range(8):
                    h = nsl * 8 + j
                    m = h // 2
                    half = 64 * (h % 2)
                    kv_bank = pb[4] if m < 8 else pb[5]
                    mc = 64 * (m % 8)
                    mme(kv_bank[half:half + 64, mc:mc + 64],
                        fmB[:, tt * D + 64 * h:tt * D + 64 * h + 64],
                        vsl[:, vofs + 64 * j:vofs + 64 * j + 64],
                        start=True, stop=True)
                    mme(pb[6][half:half + 64, m:m + 1],
                        fmB[:, tt * D + 64 * h:tt * D + 64 * h + 64],
                        c_onescol[:], start=True, stop=True)
                tv2 = tmark()
                wpend[d] = tv2
                ctx.wait("v", "st", tv2)
                ko = 256 * (nsl % 2)
                kb = 512 if nsl >= 2 else 0
                pbk = pb[5] if nsl >= 2 else pb[4]
                ctx.do("v", lambda e, kb=kb, ko=ko, pbk=pbk: e.tensor_tensor(
                    kvb[:, kb + ko:kb + ko + 256], kvb[:, kb + ko:kb + ko + 256],
                    pbk[:, ko:ko + 256], ALU.add))
                ms = 4 * nsl
                kvfree = ctx.do("v", lambda e, ms=ms: e.tensor_tensor(
                    kvb[:, 1024 + ms:1024 + ms + 4], kvb[:, 1024 + ms:1024 + ms + 4],
                    pb[6][:, ms:ms + 4], ALU.add), "sv", 1)
        ctx.wait("y", "sv")
        role_dma("y", [(shm_kv[0], kvb, False)], [(shm_kv[1], kvb, False)], sem="qkv")
        ctx.wait("g", "qy")
        ctx.wait("g", "qkv")
        sgn, sgv = pair_sync()
        ctx.wait("y", sgn, sgv)
        role_dma("y", [(kvp, shm_kv[1], False)], [(kvp, shm_kv[0], False)])
        ctx.wait("v", "qy")
        ctx.wait("v", "qkv")
        ctx.do("v", lambda e: e.tensor_tensor(kvb, kvb, kvp, ALU.add))
        ctx.do("v", lambda e: e.tensor_copy(kvf[:], kvb[:, 0:1024]))
        ctx.do("v", lambda e: e.memset(ksumS[:], 0.0))
        for ch in range(KC):
            ctx.do("v", lambda e, ch=ch: e.tensor_copy(
                ksumS[0:64, 32 * ch + 2 * ch:32 * ch + 2 * ch + 1],
                kvb[0:64, 1024 + ch:1024 + ch + 1]))
            ctx.do("v", lambda e, ch=ch: e.tensor_copy(
                ksumS[64:128, 32 * ch + 2 * ch + 1:32 * ch + 2 * ch + 2],
                kvb[64:128, 1024 + ch:1024 + ch + 1]))
        ctx.do("v", lambda e: e.drain(), "sv", 1)
        ctx.barrier()
        phase()
        wpend = [0, 0]
        evfree2 = [0, 0, 0, 0]
        for ocb in range(4):
            d = ocb % 2
            if ocb >= 2:
                ctx.wait("g", "st", wpend[d])
            qv = load_wcb("aWq", ocb, d)
            ctx.wait("t", f"qg{d}", qv)
            for tokg in range(2):
                for oc in range(4):
                    bank = pb[oc]
                    if evfree2[oc]:
                        ctx.wait("t", "ss", evfree2[oc])
                    for k in range(KC):
                        mme(bank[:], wslice(d, k, 128 * oc, 128),
                            fmA[:, k * TOK + 512 * tokg:k * TOK + 512 * tokg + 512],
                            start=(k == 0), stop=(k == KC - 1))
                tv = tmark()
                wpend[d] = tv
                ctx.wait("s", "st", tv)
                for oc in range(4):
                    chn = ocb * 4 + oc
                    evfree2[oc] = ctx.do("s", lambda e, oc=oc, chn=chn, tokg=tokg:
                                         e.activation(fmB[:, chn * TOK + 512 * tokg:chn * TOK + 512 * tokg + 512],
                                                      pb[oc][:], AF.Relu), "ss", 1)
        ctx.barrier()
        phase()
        for tokg in range(2):
            for k in range(KC):
                mme(pb[4][0:32, :], ksumS[:, 32 * k:32 * k + 32],
                    fmB[:, k * TOK + 512 * tokg:k * TOK + 512 * tokg + 512],
                    start=(k == 0), stop=(k == KC - 1))
            tv = tmark()
            ctx.wait("v", "st", tv)
            vv = ctx.do("v", lambda e, tokg=tokg: e.tensor_scalar(
                nrmv[:, 512 * tokg:512 * tokg + 512], pb[4][0:32, :], 1e-6, None, ALU.max), "sv", 1)
            ctx.wait("t", "sv", vv)
            ctx.wait("s", "sv", vv)
            ctx.do("s", lambda e, tokg=tokg: e.activation(
                nrmv[:, 512 * tokg:512 * tokg + 512], nrmv[:, 512 * tokg:512 * tokg + 512], AF.Ln))
            sv1 = ctx.do("s", lambda e, tokg=tokg: e.activation(
                nrmv[:, 512 * tokg:512 * tokg + 512], nrmv[:, 512 * tokg:512 * tokg + 512], AF.Exp, scale=-1.0), "ss", 1)
            ctx.wait("v", "ss", sv1)
            ctx.do("v", lambda e, tokg=tokg: e.tensor_copy(
                nrmvb[:, 512 * tokg:512 * tokg + 512], nrmv[:, 512 * tokg:512 * tokg + 512]), "sv", 1)
        ctx.barrier()
        phase()
        zfree = [0, 0]
        for ch in range(KC):
            for tokg in range(2):
                bank = pb[tokg]
                nbank = pb[2 + tokg]
                if zfree[tokg]:
                    ctx.wait("t", "sv", zfree[tokg])
                mme(bank[0:64, :], kvf[0:64, 64 * ch:64 * ch + 64],
                    fmB[0:64, ch * TOK + 512 * tokg:ch * TOK + 512 * tokg + 512],
                    start=True, stop=True)
                mme(bank[64:128, :], kvf[64:128, 64 * ch:64 * ch + 64],
                    fmB[64:128, ch * TOK + 512 * tokg:ch * TOK + 512 * tokg + 512],
                    start=True, stop=True)
                tv = ctx.do("t", lambda e, ch=ch, tokg=tokg, nb=nbank:
                            e.matmul(nb[:], c_esel[:, 128 * ch:128 * ch + 128],
                                     nrmvb[:, 512 * tokg:512 * tokg + 512],
                                     start=True, stop=True, skip_group_check=True), "st", 1)
                ctx.wait("v", "st", tv)
                nbs = lnt[:, 512 * tokg:512 * tokg + 512]
                ctx.do("v", lambda e, nb=nbank, nbs=nbs: e.tensor_copy(nbs, nb[:]))
                zfree[tokg] = ctx.do("v", lambda e, b=bank, ch=ch, tokg=tokg, nbs=nbs:
                                     e.tensor_tensor(fmB[:, ch * TOK + 512 * tokg:ch * TOK + 512 * tokg + 512],
                                                     b[:], nbs, ALU.mult), "sv", 1)
        ctx.barrier()
        phase()
        def bo_mm(e, b, nsl):
            return e.matmul(b[:], c_ones1x[:], c_bo[:, 512 * nsl:512 * nsl + 512],
                            start=True, stop=False, skip_group_check=True)

        def evac_a(tt, nsl, bank):
            return ctx.do("v", lambda e, b=bank, tt=tt, nsl=nsl:
                          e.scalar_tensor_tensor(xres[:, tt * D + 512 * nsl:tt * D + 512 * nsl + 512],
                                                 b[:], 1.0, xres[:, tt * D + 512 * nsl:tt * D + 512 * nsl + 512],
                                                 ALU.mult, ALU.add), "sv", 1)
        proj_tok("aWo", fmB, evac_a, extra_mm=bo_mm)

    def wq_prep():
        phase()
        ctx.do("v", lambda e: e.memset(C(40), 0.0))
        ctx.do("v", lambda e: e.memset(f32ones[:], 1.0))
        ctx.do("v", lambda e: e.memset(f32onescol[:], 1.0))
        qprev = [0, 0]
        for k in range(KC):
            d = k % 2
            half = scr[:, 2048 * d:2048 * d + 2048]
            if k >= 2:
                ctx.wait("g", "sv", qprev[d])
            qv = dma("g", half, dr["fW"][128 * k:128 * (k + 1), :], f"qf{d}")
            ctx.wait("v", f"qf{d}", qv)
            ctx.do("v", lambda e, half=half: e.tensor_reduce(C(41), half, AX.X, ALU.add, apply_absolute_value=True))
            qprev[d] = ctx.do("v", lambda e: e.tensor_tensor(C(40), C(40), C(41), ALU.add), "sv", 1)
        vv = ctx.do("v", lambda e: e.drain(), "sv", 1)
        ctx.wait("t", "sv", vv)
        tv = ctx.do("t", lambda e: e.matmul(pb[6][0:1, 0:1], C(40), f32onescol[:],
                                            start=True, stop=True, skip_group_check=True), "st", 1)
        ctx.wait("v", "st", tv)
        ctx.do("v", lambda e: e.tensor_scalar(inva[:], pb[6][0:1, 0:1], 1.0 / (D * D), 1e-8, ALU.mult, ALU.max))
        vv = ctx.do("v", lambda e: e.reciprocal(inva[:], inva[:]), "sv", 1)
        ctx.wait("t", "sv", vv)
        tv = ctx.do("t", lambda e: e.matmul(pb[6][:, 0:1], f32ones[:], inva[:],
                                            start=True, stop=True, skip_group_check=True), "st", 1)
        ctx.wait("v", "st", tv)
        ctx.do("v", lambda e: e.tensor_copy(C(42), pb[6][:, 0:1]))
        vv = ctx.do("v", lambda e: e.reciprocal(C(41), C(42)), "sv", 1)
        ctx.wait("g", "sv", vv)
        qprev = [0, 0]
        sprev = [0, 0]
        for k in range(KC):
            d = k % 2
            half = scr[:, 2048 * d:2048 * d + 2048]
            if k >= 2:
                ctx.wait("g", "sv", qprev[d])
            qv = dma("g", half, dr["fW"][128 * k:128 * (k + 1), :], f"qf{d}")
            ctx.wait("s", f"qf{d}", qv)
            ctx.do("s", lambda e, half=half: e.activation(half, half, AF.Identity,
                                                          scale=C(42), bias=C(45)))
            ssq = ctx.do("s", lambda e, half=half: e.activation(half, half, AF.Identity,
                                                                bias=C(46)), "ss", 1)
            ctx.wait("v", "ss", ssq)
            if k >= 2:
                ctx.wait("v", f"qp{d}", sprev[d])
            qprev[d] = ctx.do("v", lambda e, half=half, d=d: e.tensor_scalar(
                lnt[:, d * D:d * D + D], half, -1.0, 1.0, ALU.max, ALU.min), "sv", 1)
            ctx.wait("y", "sv", qprev[d])
            sprev[d] = dma("y", wq_hbm[128 * k:128 * (k + 1), :], lnt[:, d * D:d * D + D], f"qp{d}")
        ctx.barrier()
        phase()

    def ffn():
        ln_stage(True)

        def evac_f(tt, nsl, bank):
            return ctx.do("v", lambda e, b=bank, tt=tt, nsl=nsl:
                          e.scalar_tensor_tensor(xres[:, tt * D + 512 * nsl:tt * D + 512 * nsl + 512],
                                                 b[:], C(24 + tt), xres[:, tt * D + 512 * nsl:tt * D + 512 * nsl + 512],
                                                 ALU.mult, ALU.add), "sv", 1)
        proj_tok(None, fmA, evac_f, wq_transpose=True)

    def attn_b():
        ln_stage(False)
        ropev = lambda tt: c_ropec[:, 64 * tt:64 * tt + 64].unsqueeze(1).broadcast_to([128, 4, 64])
        ropesv = lambda tt: c_ropes[:, 64 * tt:64 * tt + 64].unsqueeze(1).broadcast_to([128, 4, 64])

        def h4(ap):
            return ap.rearrange("p (h d) -> p h d", h=4)

        stq = [0, 0]
        wpend = [0, 0]
        evfree = [0, 0]
        for wname, target, kind in (("bWv", shm_v, "v"), ("bWk", shm_k, "k"), ("bWq", shm_q, "q")):
            for nsl in range(4):
                d = nsl % 2
                if wpend[d]:
                    ctx.wait("g", "st", wpend[d])
                qv = load_wcb(wname, nsl, d)
                ctx.wait("t", f"qg{d}", qv)
                for tt in range(NT):
                    bank = pb[tt % 2]
                    if evfree[tt % 2]:
                        ctx.wait("t", "sv", evfree[tt % 2])
                    if stq[tt % 2]:
                        ctx.wait("v", f"qs{tt % 2}", stq[tt % 2])
                        ctx.wait("s", f"qs{tt % 2}", stq[tt % 2])
                    for k in range(KC):
                        mme(bank[:], fmA[:, k * TOK + 128 * tt:k * TOK + 128 * tt + 128],
                            wslice(d, k), start=(k == 0), stop=(k == KC - 1))
                    tv = tmark()
                    wpend[d] = tv
                    sA = lnt[:, (tt % 2) * D:(tt % 2) * D + 512]
                    sB = lnt[:, (tt % 2) * D + 512:(tt % 2) * D + 1024]
                    sC = lnt[:, (tt % 2) * D + 1024:(tt % 2) * D + 1536]
                    sD = lnt[:, (tt % 2) * D + 1536:(tt % 2) * D + 2048]
                    if kind == "v":
                        ctx.wait("v", "st", tv)
                        vv = ctx.do("v", lambda e, b=bank, sA=sA: e.tensor_copy(sA, b[:]), "sv", 1)
                        evfree[tt % 2] = vv
                        ctx.wait("y", "sv", vv)
                        stq[tt % 2] = role_dma("y",
                                 [(target[0][128 * tt:128 * tt + 128, 512 * nsl:512 * nsl + 512], sA, False)],
                                 [(target[1][128 * tt:128 * tt + 128, 512 * nsl:512 * nsl + 512], sA, False)],
                                 sem=f"qs{tt % 2}")
                    else:
                        if kind == "k":
                            ctx.wait("s", "st", tv)
                            sse = ctx.do("s", lambda e, b=bank: e.activation(scr[:, 3072:3584], b[:], AF.Square), "ss", 1)
                            ctx.wait("v", "ss", sse)
                            ctx.do("v", lambda e: e.tensor_reduce(
                                scr2[:, 128:132], scr[:, 3072:3584].rearrange("p (h d) -> p h d", h=4), AX.X, ALU.add))
                            vv = ctx.do("v", lambda e: e.drain(), "sv", 1)
                            ctx.wait("s", "sv", vv)
                            sse = ctx.do("s", lambda e: e.activation(scr2[:, 128:132], scr2[:, 128:132], AF.Sqrt, bias=C(44)), "ss", 1)
                            ctx.wait("v", "ss", sse)
                            ctx.do("v", lambda e: e.reciprocal(scr2[:, 128:132], scr2[:, 128:132]))
                            for hh in range(4):
                                ctx.do("v", lambda e, b=bank, hh=hh, sA=sA: e.tensor_scalar(
                                    sA[:, 128 * hh:128 * hh + 128], b[:, 128 * hh:128 * hh + 128],
                                    scr2[:, 128 + hh:129 + hh], None, ALU.mult))
                        else:
                            ctx.wait("v", "st", tv)
                            ctx.do("v", lambda e, b=bank, sA=sA: e.tensor_copy(sA, b[:]))
                        x1 = h4(sA)[:, :, 0:64]
                        x2 = h4(sA)[:, :, 64:128]
                        o1 = h4(sB)[:, :, 0:64]
                        o2 = h4(sB)[:, :, 64:128]
                        t1 = h4(sC)[:, :, 0:64]
                        t2 = h4(sD)[:, :, 0:64]
                        ctx.do("v", lambda e, x1=x1, tt=tt, t1=t1: e.tensor_tensor(t1, x1, ropev(tt), ALU.mult))
                        ctx.do("v", lambda e, x2=x2, tt=tt, t2=t2: e.tensor_tensor(t2, x2, ropesv(tt), ALU.mult))
                        ctx.do("v", lambda e, t1=t1, t2=t2, o1=o1: e.tensor_tensor(o1, t1, t2, ALU.subtract))
                        ctx.do("v", lambda e, x1=x1, tt=tt, t1=t1: e.tensor_tensor(t1, x1, ropesv(tt), ALU.mult))
                        ctx.do("v", lambda e, x2=x2, tt=tt, t2=t2: e.tensor_tensor(t2, x2, ropev(tt), ALU.mult))
                        vv = ctx.do("v", lambda e, t1=t1, t2=t2, o2=o2: e.tensor_tensor(o2, t1, t2, ALU.add), "sv", 1)
                        evfree[tt % 2] = vv
                        ctx.wait("y", "sv", vv)
                        stq[tt % 2] = role_dma("y",
                                 [(target[0][128 * tt:128 * tt + 128, 512 * nsl:512 * nsl + 512], sB, False)],
                                 [(target[1][128 * tt:128 * tt + 128, 512 * nsl:512 * nsl + 512], sB, False)],
                                 sem=f"qs{tt % 2}")
        ctx.barrier()
        bqy = 0
        for tt in range(NT):
            if bqy:
                ctx.wait("v", "qb", bqy)
                ctx.wait("s", "qb", bqy)
            for k in range(KC):
                mme(pb[6][:, 0:16], fmA[:, k * TOK + 128 * tt:k * TOK + 128 * tt + 128],
                    c_wb[:, 16 * k:16 * k + 16], start=(k == 0), stop=(k == KC - 1))
            tv = tmark()
            ctx.wait("s", "st", tv)
            sse = ctx.do("s", lambda e: e.activation(scr2[:, 256:272], pb[6][:, 0:16], AF.Exp, scale=-1.0), "ss", 1)
            ctx.wait("v", "ss", sse)
            ctx.do("v", lambda e: e.tensor_scalar(scr2[:, 256:272], scr2[:, 256:272], 1.0, None, ALU.add))
            ctx.do("v", lambda e: e.reciprocal(scr2[:, 256:272], scr2[:, 256:272]))
            vv = ctx.do("v", lambda e: e.tensor_scalar(scr2[:, 272:288], scr2[:, 256:272], 2.0, None, ALU.mult), "sv", 1)
            ctx.wait("t", "sv", vv)
            vv2 = ctx.do("v", lambda e: e.tensor_copy(lnt[:, 0:16], scr2[:, 272:288]), "sv", 1)
            ctx.wait("t", "sv", vv2)
            tv = ctx.do("t", lambda e: e.transpose(pbt_b[0:16, 0:128], lnt[:, 0:16], c_id[:]), "st", 1)
            ctx.wait("v", "st", tv)
            vv3 = ctx.do("v", lambda e: e.tensor_copy(lnt[0:16, 256:384], pbt_b[0:16, 0:128]), "sv", 1)
            ctx.wait("y", "sv", vv3)
            bqy = role_dma("y",
                     [(shm_bt[0][128 * tt:128 * tt + 128, :], scr2[:, 272:288], False),
                      (shm_br[0][:, 128 * tt:128 * tt + 128], lnt[0:16, 256:384], False)],
                     [(shm_bt[1][128 * tt:128 * tt + 128, :], scr2[:, 272:288], False),
                      (shm_br[1][:, 128 * tt:128 * tt + 128], lnt[0:16, 256:384], False)],
                     sem="qb")
        ctx.barrier()
        ctx.wait("g", "qy")
        ctx.wait("g", "qs0")
        ctx.wait("g", "qs1")
        ctx.wait("g", "qb")
        sgp = pair_sync()
        attn_b_mix(sgp)
        phase()
        ctx.wait("g", "qy")
        ctx.wait("g", "qo")
        sg2n, sg2v = pair_sync()
        ctx.wait("y", sg2n, sg2v)
        ctx.wait("y", "qo")
        p0 = []
        p1 = []
        for s in range(2):
            for hl in range(8):
                chn = 8 * s + hl
                dst = fmA[:, chn * TOK:(chn + 1) * TOK]
                p0.append((dst, shm_o[s][hl][:, 0:TOK], False))
                p1.append((dst, shm_o[s][hl][:, TOK:2 * TOK], False))
        role_dma("y", p0, p1)
        ctx.wait("t", "qy")
        ctx.barrier()

        def evac_b(tt, nsl, bank):
            return ctx.do("v", lambda e, b=bank, tt=tt, nsl=nsl:
                          e.scalar_tensor_tensor(xres[:, tt * D + 512 * nsl:tt * D + 512 * nsl + 512],
                                                 b[:], 1.0, xres[:, tt * D + 512 * nsl:tt * D + 512 * nsl + 512],
                                                 ALU.mult, ALU.add), "sv", 1)
        proj_tok("bWo", fmA, evac_b)

    def attn_b_mix(sgp):
        ctx.wait("y", sgp[0], sgp[1])
        ctx.wait("y", "qs0")
        ctx.wait("y", "qs1")
        ctx.wait("y", "qb")
        for s in range(2):
            dma("y", betat[:, 128 * s:128 * (s + 1)].rearrange("p (c h) -> p c h", c=8),
                shm_bt[s].rearrange("(c p) h -> p c h", p=128), "qbt")
        ctx.wait("v", "qbt")
        POWP = [((2, 1, 0), (3, 0, 1)), ((4, 3, 2), (5, 2, 3)), ((6, 5, 4), (7, 4, 5)),
                ((8, 7, 6), (9, 6, 7)), ((10, 9, 8), (11, 8, 9)), (((12, 11, 10)),)]
        HORD = [12, 11, 9, 7, 5, 3, 1]

        def powslot(c, orig, w=128):
            return fmB[:, 8192 + 1664 * (c % 2) + 128 * orig:
                       8192 + 1664 * (c % 2) + 128 * orig + w]

        def XA(c):
            return fmB[:, 11648 + 512 * (c % 3):11648 + 512 * (c % 3) + 256]

        def XB(c):
            return fmB[:, 11648 + 512 * (c % 3) + 256:11648 + 512 * (c % 3) + 512]

        def A_ukt(c):
            return fmA[:, 14336 + 128 * c:14336 + 128 * c + 128]

        def emit_loads(i, batch):
            p0 = []
            p1 = []
            for s in range(2):
                for role_, plist, hg in ((0, p0, i), (1, p1, 8 + i)):
                    if batch == 2:
                        plist.append((A_qhT[:, 1024 * s:1024 * s + 1024],
                                      shm_q[s][:, 128 * hg:128 * hg + 128], True))
                        continue
                    plist.append((A_khT[:, 1024 * s:1024 * s + 1024],
                                  shm_k[s][:, 128 * hg:128 * hg + 128], True))
                    plist.append((A_ktok[:, 1024 * s:1024 * s + 1024].rearrange("p (c d) -> p c d", c=8),
                                  shm_k[s][:, 128 * hg:128 * hg + 128].rearrange("(c p) d -> p c d", p=128), False))
                    plist.append((A_vtok[:, 1024 * s:1024 * s + 1024].rearrange("p (c d) -> p c d", c=8),
                                  shm_v[s][:, 128 * hg:128 * hg + 128].rearrange("(c p) d -> p c d", p=128), False))
                    plist.append((betarow[0:1, 1024 * s:1024 * s + 1024],
                                  shm_br[s][hg:hg + 1, :], False))
            return role_dma("y", p0, p1)

        seq_tdone = [0]
        qv_next = [None]
        for i in range(NSEQ):
            if seq_tdone[0]:
                ctx.wait("v", "st", seq_tdone[0])
                ctx.wait("s", "st", seq_tdone[0])
            if qv_next[0] is None:
                if seq_tdone[0]:
                    ctx.wait("y", "st", seq_tdone[0])
                    ctx.wait("y", "sv")
                emit_loads(i, 1)
                qv = emit_loads(i, 2)
            else:
                qv = qv_next[0]
            svs = ctx.sem("sv")
            prevv = ctx.cnt["sv"]
            ctx.cnt["sv"] = prevv + 1

            def bc(e, i=i, prevv=prevv, svs=svs):
                if prevv:
                    e.wait_ge(svs, prevv)
                rr = ctx.regs["v_role"]
                bt3 = betat.rearrange("p (c h) -> p c h", h=16)
                dst3 = betacol.unsqueeze(2)
                with e.If_eq(rr, 0):
                    e.tensor_copy(dst3, bt3[:, :, i:i + 1]).then_inc(svs, 1)
                with e.Else():
                    e.tensor_copy(dst3, bt3[:, :, 8 + i:8 + i + 1]).then_inc(svs, 1)
            ctx.wait("v", "qy", qv)
            ctx.on("v", bc)
            ctx.wait("t", "qy", qv)
            ktTrdy = [0] * 4
            for g4 in range(4):
                tv = ctx.do("t", lambda e, g4=g4: e.matmul(pb[5][:], c_ones1x[:],
                                                           betarow[0:1, 512 * g4:512 * g4 + 512],
                                                           start=True, stop=True, skip_group_check=True), "st", 1)
                ctx.wait("v", "st", tv)
                vv = ctx.do("v", lambda e, g4=g4: e.tensor_tensor(
                    A_ktT[:, 512 * g4:512 * g4 + 512], A_khT[:, 512 * g4:512 * g4 + 512],
                    pb[5][:], ALU.mult), "sv", 1)
                ctx.wait("t", "sv", vv)
                ktTrdy[g4] = vv
            tsv = [0] * NCH
            for c in range(16):
                tsv[c] = ctx.do("v", lambda e, c=c: e.tensor_scalar(
                    A_kttok[:, 128 * c:128 * c + 128], A_ktok[:, 128 * c:128 * c + 128],
                    betacol[:, c:c + 1], None, ALU.mult), "sv", 1)
            ctx.do("v", lambda e: e.memset(S32, 0.0))
            msv = ctx.do("v", lambda e: e.memset(B_S16, 0.0), "sv", 1)

            aevac = [0] * NCH
            powrdy = {}
            bankfree = {2: ("ss", 0), 3: ("ss", 0)}
            chmm = [[0] * 7 for _ in range(NCH)]
            chtt = [[0] * 7 for _ in range(NCH)]
            xcp = [0] * NCH
            trst = [0] * NCH
            uktv = [0] * NCH
            m1st = [0] * NCH
            m2st = [0] * NCH
            vnsv = [0] * NCH
            sttv = [0] * NCH
            s32v = [0] * NCH
            expss = {}
            masksv = {}
            sbank = [0, 0]
            avfree = [0]
            avdone = [None]

            def emit_xcopy(c):
                if c >= 3:
                    ctx.wait("v", "st", chmm[c - 3][6])
                ctx.do("v", lambda e, c=c: e.tensor_copy(XA(c)[:, 0:128], A_vtok[:, 128 * c:128 * c + 128]))
                xcp[c] = ctx.do("v", lambda e, c=c: e.tensor_copy(XA(c)[:, 128:256], A_ktok[:, 128 * c:128 * c + 128]), "sv", 1)

            def emit_amm(c):
                if c >= 1:
                    ctx.wait("t", "sv", aevac[c - 1])
                ctx.wait("t", "sv", ktTrdy[c // 4])
                kh = A_khT[:, 128 * c:128 * c + 128]
                kt = A_ktT[:, 128 * c:128 * c + 128]
                mme(pb[0][:, 0:128], kh, kt, True, True)
                tv = mme(pb[0][:, 128:256], kt, kh, True, True, inc=True)
                ctx.wait("v", "st", tv)
                if c >= 2:
                    ctx.wait("v", "st", chmm[c - 2][6])
                vv = ctx.do("v", lambda e, c=c: e.tensor_tensor(
                    powslot(c, 0, 256), pb[0][:, 0:256], c_trilu[:], ALU.mult), "sv", 1)
                aevac[c] = vv
                powrdy[(c, 0)] = ("sv", vv)
                powrdy[(c, 1)] = ("sv", vv)

            def emit_pow(c, pp):
                pair = POWP[pp]
                bk = 2 + (pp % 2)
                fn, fv = bankfree[bk]
                ctx.wait("t", fn, fv)
                deps = set()
                for ent in pair:
                    deps.add(ent[1])
                    deps.add(ent[2])
                for o_ in sorted(deps):
                    n_, v_ = powrdy[(c, o_)]
                    ctx.wait("t", n_, v_)
                for ii, (dst, lh, rh) in enumerate(pair):
                    last = ii == len(pair) - 1
                    tv = mme(pb[bk][:, 128 * ii:128 * ii + 128],
                             powslot(c, lh), powslot(c, rh), True, True, inc=last)
                ctx.wait("s", "st", tv)
                if pp == 0 and c >= 2:
                    ctx.wait("s", "st", chmm[c - 2][6])
                w = 128 * len(pair)
                d0 = pair[0][0]
                sv_ = ctx.do("s", lambda e, c=c, d0=d0, bk=bk, w=w: e.activation(
                    powslot(c, d0, w), pb[bk][:, 0:w], AF.Identity), "ss", 1)
                for (dst, lh, rh) in pair:
                    powrdy[(c, dst)] = ("ss", sv_)
                bankfree[bk] = ("ss", sv_)

            def emit_chain(c, j):
                half = pb[1][:, 256 * (c % 2):256 * (c % 2) + 256]
                if j == 0:
                    ctx.wait("t", "sv", xcp[c])
                    if c >= 2:
                        ctx.wait("t", "sv", chtt[c - 2][6])
                else:
                    ctx.wait("t", "sv", chtt[c][j - 1])
                n_, v_ = powrdy[(c, HORD[j])]
                ctx.wait("t", n_, v_)
                xc = XA(c) if j % 2 == 0 else XB(c)
                xn = XB(c) if j % 2 == 0 else XA(c)
                if j == 6:
                    xn = XB(c)
                chmm[c][j] = mme(half, powslot(c, HORD[j]), xc, True, True, inc=True)
                ctx.wait("v", "st", chmm[c][j])
                op = ALU.subtract if j == 6 else ALU.add
                chtt[c][j] = ctx.do("v", lambda e, xn=xn, xc=xc, half=half, op=op:
                                    e.tensor_tensor(xn, xc, half, op), "sv", 1)

            def emit_transpose(c):
                ctx.wait("t", "sv", chtt[c][6])
                if c >= 8:
                    ctx.wait("t", "sv", uktv[c - 8])
                trst[c] = ctx.do("t", lambda e, c=c: e.transpose(
                    pbt_b[0:128, 128 * (c % 8):128 * (c % 8) + 128], XB(c)[:, 128:256], c_id[:]), "st", 1)
                ctx.wait("v", "st", trst[c])
                uktv[c] = ctx.do("v", lambda e, c=c: e.tensor_copy(
                    A_ukt(c), pbt_b[0:128, 128 * (c % 8):128 * (c % 8) + 128]), "sv", 1)

            def emit_seq(c):
                if _REG_BYPASS:
                    vnsv[c] = ctx.do("v", lambda e, c=c: e.tensor_copy(
                        A_vn[:, 128 * c:128 * c + 128], A_vtok[:, 128 * c:128 * c + 128]), "sv", 1)
                    return
                ctx.wait("t", "sv", uktv[c])
                ctx.wait("t", "sv", sttv[c - 1] if c >= 1 else msv)
                if c >= 1:
                    ctx.wait("t", "sv", vnsv[c - 1])
                m1st[c] = mme(pb[4][:, 0:128], A_ukt(c), B_S16, True, True, inc=True)
                ctx.wait("v", "st", m1st[c])
                vnsv[c] = ctx.do("v", lambda e, c=c: e.tensor_tensor(
                    A_vn[:, 128 * c:128 * c + 128], XB(c)[:, 0:128], pb[4][:, 0:128], ALU.subtract), "sv", 1)
                ctx.wait("t", "sv", vnsv[c])
                ctx.wait("t", "sv", tsv[c])
                if c >= 1:
                    ctx.wait("t", "sv", s32v[c - 1])
                m2st[c] = mme(pb[4][:, 128:256], A_kttok[:, 128 * c:128 * c + 128],
                              A_vn[:, 128 * c:128 * c + 128], True, True, inc=True)
                ctx.wait("v", "st", m2st[c])
                sttv[c] = ctx.do("v", lambda e: e.scalar_tensor_tensor(
                    B_S16, pb[4][:, 128:256], 1.0, S32, ALU.mult, ALU.add), "sv", 1)
                s32v[c] = ctx.do("v", lambda e: e.tensor_tensor(
                    S32, S32, pb[4][:, 128:256], ALU.add), "sv", 1)

            def emit_smm(qg, kt):
                bank = pb[5 + (kt % 2)]
                if sbank[kt % 2]:
                    ctx.wait("t", "ss", sbank[kt % 2])
                if avdone[0] is not None:
                    ctx.wait("t", "sv", avdone[0])
                tv = mme(bank[:], A_ktT[:, 128 * kt:128 * kt + 128],
                         A_qhT[:, 512 * qg:512 * qg + 512], True, True, inc=True)
                ctx.wait("s", "st", tv)
                sv_ = ctx.do("s", lambda e, kt=kt, b=bank: e.activation(
                    B_exp(kt), b[:], AF.Exp, scale=SCALE_B), "ss", 1)
                expss[kt] = sv_
                sbank[kt % 2] = sv_
                if kt >= 4 * qg:
                    o = kt - 4 * qg
                    ctx.wait("v", "ss", sv_)
                    if o > 0:
                        ctx.do("v", lambda e, kt=kt, o=o: e.memset(B_exp(kt)[:, 0:128 * o], 0.0))
                    masksv[kt] = ctx.do("v", lambda e, kt=kt, o=o: e.tensor_tensor(
                        B_exp(kt)[:, 128 * o:128 * o + 128], B_exp(kt)[:, 128 * o:128 * o + 128],
                        c_triu_incl[:], ALU.mult), "sv", 1)

            def emit_attn(qg):
                nkt = 4 * qg + 4
                fn2, fv2 = bankfree[2]
                ctx.wait("t", fn2, fv2)
                fn3, fv3 = bankfree[3]
                ctx.wait("t", fn3, fv3)
                if avfree[0]:
                    ctx.wait("t", "sv", avfree[0])
                for kt in range(nkt):
                    ctx.wait("t", "ss", expss[kt])
                    if kt in masksv:
                        ctx.wait("t", "sv", masksv[kt])
                    ctx.wait("t", "sv", vnsv[kt])
                    mme(pb[2][:], A_vn[:, 128 * kt:128 * kt + 128], B_exp(kt),
                        kt == 0, kt == nkt - 1)
                    mme(pb[3][0:1, :], c_onescol[:], B_exp(kt),
                        kt == 0, kt == nkt - 1)
                tv = tmark()
                ctx.wait("v", "st", tv)
                ctx.do("v", lambda e, qg=qg: e.tensor_copy(scr[:, 512 * qg:512 * qg + 512], pb[2][:]))
                vv = ctx.do("v", lambda e, qg=qg: e.tensor_copy(nrow[0:1, 512 * qg:512 * qg + 512], pb[3][0:1, :]), "sv", 1)
                avfree[0] = vv
                avdone[0] = vv
                bankfree[2] = ("sv", vv)
                bankfree[3] = ("sv", vv)
                masksv.clear()

            if _REG_BYPASS:
                for c in range(NCH):
                    emit_seq(c)
                for qg in range(4):
                    nkt = 4 * qg + 4
                    for kt in range(nkt):
                        emit_smm(qg, kt)
                    emit_attn(qg)
            else:
                smm_sched = {}
                for qg in range(4):
                    kts = list(range(4 * qg + 4))
                    st0 = 4 * qg + 2 if qg else 1
                    nstg = 2 if qg == 3 else 4
                    for idx, kt in enumerate(kts):
                        smm_sched.setdefault(st0 + idx % nstg, []).append((qg, kt))
                for k in range(18):
                    if k <= 15:
                        emit_xcopy(k)
                    smms = list(smm_sched.get(k, []))

                    def pop_smm():
                        if smms:
                            qg_, kt_ = smms.pop(0)
                            emit_smm(qg_, kt_)
                    if k == 0:
                        emit_amm(0)
                        for pp in range(6):
                            emit_pow(0, pp)
                    elif k <= 16:
                        c = k - 1
                        pows = ([("a",)] + [("p", pp) for pp in range(6)]) if k <= 15 else []
                        wseq = []
                        for j in range(7):
                            wseq.append(("c", j))
                            wseq.append(None)
                            wseq.append(None)
                        pi2 = 0
                        for slot_i, s_ in enumerate(wseq):
                            if s_ is None:
                                if pi2 < len(pows):
                                    it = pows[pi2]
                                    pi2 += 1
                                    if it[0] == "a":
                                        emit_amm(k)
                                    else:
                                        emit_pow(k, it[1])
                                else:
                                    pop_smm()
                            elif s_[0] == "c":
                                emit_chain(c, s_[1])
                                pop_smm()
                        while pi2 < len(pows):
                            it = pows[pi2]
                            pi2 += 1
                            if it[0] == "a":
                                emit_amm(k)
                            else:
                                emit_pow(k, it[1])
                    while smms:
                        pop_smm()
                    if 1 <= k <= 16:
                        emit_transpose(k - 1)
                    if k == 16 and i + 1 < NSEQ:
                        ctx.wait("y", "sv", aevac[15])
                        ctx.wait("y", "sv", xcp[15])
                        emit_loads(i + 1, 1)
                        ctx.wait("y", "ss", expss[15])
                        qv_next[0] = emit_loads(i + 1, 2)
                    if 2 <= k <= 17:
                        emit_seq(k - 2)
                    if k in (5, 9, 13, 17):
                        emit_attn((k - 5) // 4)
            ctx.wait("s", "sv", avfree[0])
            ctx.do("s", lambda e: e.activation(nrow, nrow, AF.Ln))
            sse = ctx.do("s", lambda e: e.activation(nrow, nrow, AF.Exp, scale=-1.0), "ss", 1)
            ctx.wait("v", "ss", sse)
            vv = ctx.do("v", lambda e: e.tensor_copy(nrowb, nrow), "sv", 1)
            ctx.wait("t", "sv", vv)
            ctx.wait("t", "ss", sbank[0])
            ctx.wait("t", "ss", sbank[1])
            for qg in range(4):
                tv = ctx.do("t", lambda e, qg=qg: e.matmul(pb[5][:], c_ones1x[:],
                                                           nrowb[0:1, 512 * qg:512 * qg + 512],
                                                           start=True, stop=True, skip_group_check=True), "st", 1)
                ctx.wait("v", "st", tv)
                vv = ctx.do("v", lambda e, qg=qg: e.tensor_tensor(
                    B_oT[:, 512 * qg:512 * qg + 512], scr[:, 512 * qg:512 * qg + 512],
                    pb[5][:], ALU.mult), "sv", 1)
                ctx.wait("t", "sv", vv)
            ctx.wait("y", "sv")
            role_dma("y", [(shm_o[0][i], B_oT, False)], [(shm_o[1][i], B_oT, False)], sem="qo")
            ctx.wait("v", "qo")
            seq_tdone[0] = tmark()

    pbt_b = stack.enter_context(nc.psum_tensor("pbt_b", [128, 1024], BF16))

    try:
        attn_a()
        if nsb >= 2:
            wq_prep()
            ffn()
        if nsb >= 3:
            attn_b()
        if nsb >= 4:
            ffn()
    except _StopEmit:
        ctx.barrier()

    ctx.wait("y", "sv")
    ctx.wait("y", "st")
    ctx.wait("y", "ss")
    for i in range(NT):
        dma("y", y_out[128 * i:128 * (i + 1), :], xres[:, i * D:(i + 1) * D], "qy")
    ctx.wait("y", "qy")

    RT = 32 if sim else 16
    with nc.Block() as block:
        @block.sync
        def _(e):
            e.dma_start(out=srole[:], in_=dr["role"][:]).then_inc(ctx.sem("qy_role"), 16)
            if sim:
                e.dma_start(out=sppnc[:], in_=dr["ppnc"][:]).then_inc(ctx.sem("qy_role"), 16)
            e.wait_ge(ctx.sem("qy_role"), RT)
            r = e.alloc_register()
            e.load(r, srole[0:1, 0:1])
            ctx.regs["y_role"] = r
            for f in ctx.ops["y"]:
                f(e)

        @block.gpsimd
        def _(e):
            if sim:
                e.load_library(library_config.remote_dma)
            e.wait_ge(ctx.sem("qy_role"), RT)
            r = e.alloc_register()
            e.load(r, srole[0:1, 0:1])
            ctx.regs["g_role"] = r
            if sim:
                pp = e.alloc_register()
                e.load(pp, sppnc[0:1, 0:1])
                ctx.regs["g_ppnc"] = pp
            for f in ctx.ops["g"]:
                f(e)

        @block.tensor
        def _(e):
            for f in ctx.ops["t"]:
                f(e)

        @block.vector
        def _(e):
            e.wait_ge(ctx.sem("qy_role"), RT)
            r = e.alloc_register()
            e.load(r, srole[0:1, 0:1])
            ctx.regs["v_role"] = r
            for f in ctx.ops["v"]:
                f(e)

        @block.scalar
        def _(e):
            for f in ctx.ops["s"]:
                f(e)

    stack.close()
    return nc



def _host_consts(role, core=None):
    if core is None:
        core = role
    c = {}
    my_pnc = node_relative_to_routing_address(core % 8, 8)[1]
    c["ppnc"] = np.array([[my_pnc ^ 1]], dtype=np.int32)
    c["ident"] = np.eye(128, dtype=BF)
    i = np.arange(128)
    c["trilmask"] = (i[:, None] > i[None, :]).astype(BF)
    c["triumask"] = (i[:, None] < i[None, :]).astype(BF)
    c["triuincl"] = (i[:, None] <= i[None, :]).astype(BF)
    c["trilu"] = np.concatenate([(i[:, None] > i[None, :]).astype(BF),
                                 (i[:, None] < i[None, :]).astype(BF)], axis=1)
    es = np.zeros((32, KC * 128), dtype=np.float32)
    for ch in range(KC):
        es[2 * ch, 128 * ch:128 * ch + 64] = 1.0
        es[2 * ch + 1, 128 * ch + 64:128 * ch + 128] = 1.0
    c["esel"] = es.astype(BF)
    c["ones1x"] = np.ones((1, 128), dtype=BF)
    c["onescol"] = np.ones((128, 1), dtype=BF)
    pos = (TOK * role + np.arange(TOK)).astype(np.float32)
    inv = (1.0 / (10000.0 ** (np.arange(0, 128, 2, dtype=np.float32) / np.float32(128.0)))).astype(np.float32)
    th = pos[:, None] * inv[None, :]
    c["ropec"] = np.cos(th).astype(np.float32).reshape(NT, 128, 64).transpose(1, 0, 2).reshape(128, NT * 64).astype(BF)
    c["ropes"] = np.sin(th).astype(np.float32).reshape(NT, 128, 64).transpose(1, 0, 2).reshape(128, NT * 64).astype(BF)
    c["role"] = np.array([[role]], dtype=np.int32)
    return c



def kernel(L0_attn_a_hidden_states, L0_attn_a_Wq, L0_attn_a_Wk, L0_attn_a_Wv,
           L0_attn_a_Wo, L0_attn_a_bo, L0_attn_b_Wq, L0_attn_b_Wk, L0_attn_b_Wv,
           L0_attn_b_Wo, L0_attn_b_Wb, L0_ffn_weight):
    x = np.ascontiguousarray(np.asarray(L0_attn_a_hidden_states, dtype=np.float32))
    weights = {
        "aWq": np.ascontiguousarray(np.asarray(L0_attn_a_Wq, dtype=np.float32)),
        "aWk": np.ascontiguousarray(np.asarray(L0_attn_a_Wk, dtype=np.float32)),
        "aWv": np.ascontiguousarray(np.asarray(L0_attn_a_Wv, dtype=np.float32)),
        "aWo": np.ascontiguousarray(np.asarray(L0_attn_a_Wo, dtype=np.float32)),
        "bWq": np.ascontiguousarray(np.asarray(L0_attn_b_Wq, dtype=np.float32)),
        "bWk": np.ascontiguousarray(np.asarray(L0_attn_b_Wk, dtype=np.float32)),
        "bWv": np.ascontiguousarray(np.asarray(L0_attn_b_Wv, dtype=np.float32)),
        "bWo": np.ascontiguousarray(np.asarray(L0_attn_b_Wo, dtype=np.float32)),
        "fW": np.ascontiguousarray(np.asarray(L0_ffn_weight, dtype=np.float32)),
        "bWb": np.ascontiguousarray(np.asarray(L0_attn_b_Wb, dtype=np.float32)),
        "abo": np.ascontiguousarray(np.asarray(L0_attn_a_bo, dtype=np.float32))[None, :],
    }
    key = (_SUBBLOCKS, _PHASES, _is_sim())
    if key not in _CACHE:
        _CACHE[key] = build_program(_SUBBLOCKS)
    nc = _CACHE[key]
    in_maps = []
    for c in range(8):
        b, r = c // 2, c % 2
        m = dict(weights)
        m["x"] = np.ascontiguousarray(x[b, TOK * r:TOK * (r + 1), :])
        m.update(_host_consts(r, core=c))
        in_maps.append(m)
    res = run_bass_kernel_spmd(nc, in_maps, core_ids=list(range(8)))
    out = np.empty((4, 2048, D), dtype=np.float32)
    for c in range(8):
        b, r = c // 2, c % 2
        out[b, TOK * r:TOK * (r + 1), :] = res.results[c]["y"]
    return out
